# version 27
# baseline (speedup 1.0000x reference)
import numpy as np
import ml_dtypes
from contextlib import ExitStack

import concourse.bass as bass
import concourse.mybir as mybir
from concourse.bass_utils import run_bass_kernel_spmd

F32 = mybir.dt.float32
BF16 = mybir.dt.bfloat16
U32 = mybir.dt.uint32
AF = mybir.ActivationFunctionType
ALU = mybir.AluOpType
AX = mybir.AxisListType

NCORE = 8
P = 128
D = 1024
TOK = 2048
TC = 256
NCH = TOK // TC
CB = 8
DBLK = 8
EPS = 1e-5

ENGS = ("T", "V", "S", "Q", "G")

_DETECTOR_SAFE = False


class _Op:
    __slots__ = ("eng", "emit", "reads", "writes", "deps", "inc", "idx",
                 "val", "waits", "custom_sem", "amount", "stream", "issue_val")

    def __init__(self, eng, emit, reads, writes, custom_sem=None):
        self.eng = eng
        self.stream = None
        self.emit = emit
        self.reads = tuple(reads)
        self.writes = tuple(writes)
        self.custom_sem = custom_sem
        self.deps = None
        self.inc = False
        self.val = None
        self.waits = None
        self.amount = 16 if eng in ("Q",) else 1


class Prog:
    def __init__(self):
        self.ops = []

    def add(self, eng, emit, reads=(), writes=(), custom_sem=None, amount=None,
            stream=None):
        op = _Op(eng, emit, reads, writes, custom_sem)
        if amount is not None:
            op.amount = amount
        if eng == "Q":
            op.stream = stream or "qs"
        op.idx = len(self.ops)
        self.ops.append(op)
        return op

    def finalize(self):
        last_writer = {}
        readers = {}
        for op in self.ops:
            deps = set()
            for r in op.reads:
                w = last_writer.get(r)
                if w is not None and w is not op:
                    deps.add(w)
            for w in op.writes:
                pw = last_writer.get(w)
                if pw is not None and pw is not op:
                    deps.add(pw)
                for rd in readers.get(w, ()):
                    if rd is not op:
                        deps.add(rd)
            for r in op.reads:
                readers.setdefault(r, []).append(op)
            for w in op.writes:
                last_writer[w] = op
                readers[w] = []
            op.deps = [d for d in deps
                       if not (d.eng == op.eng and op.eng == "T")
                       and not (op.eng == "Q" and d.eng == "Q"
                                and d.stream == op.stream)]
        for op in self.ops:
            for d in op.deps:
                if d.custom_sem is None:
                    d.inc = True
        counters = {e: 0 for e in ENGS}
        qcounters = {}
        for op in self.ops:
            if op.eng == "Q":
                op.inc = True
                qcounters[op.stream] = qcounters.get(op.stream, 0) + op.amount
                op.val = qcounters[op.stream]
                op.issue_val = op.val
            elif op.custom_sem is not None:
                op.val = None
            elif op.inc:
                counters[op.eng] += op.amount
                op.val = counters[op.eng]
            else:
                op.val = None
        qs_total = qcounters.get("qs", 0)
        for op in self.ops:
            if op.eng == "Q" and op.stream == "qs":
                op.val = qs_total
        marks = {e: {f: 0 for f in ENGS} for e in ENGS}
        cmarks = {e: {} for e in ENGS}
        for op in self.ops:
            waits = []
            need = {}
            cneed = {}
            for d in op.deps:
                if d.custom_sem is not None:
                    nm, v = d.custom_sem
                    cneed[nm] = max(cneed.get(nm, 0), v)
                elif d.eng == "Q":
                    cneed[d.stream] = max(cneed.get(d.stream, 0), d.val)
                else:
                    assert d.val is not None
                    need[d.eng] = max(need.get(d.eng, 0), d.val)
            for e, v in sorted(need.items()):
                if v > marks[op.eng][e]:
                    waits.append((e, v))
                    marks[op.eng][e] = v
            for nm, v in sorted(cneed.items()):
                if v > cmarks[op.eng].get(nm, 0):
                    waits.append((nm, v))
                    cmarks[op.eng][nm] = v
            op.waits = waits

    def emit_engine(self, eng_name, eng, sems):
        for op in self.ops:
            if op.eng != eng_name:
                continue
            for tgt, v in op.waits:
                eng.wait_ge(sems[tgt], v)
            if _DETECTOR_SAFE and op.eng == "Q" and op.issue_val > op.amount:
                eng.wait_ge(sems[op.stream], op.issue_val - op.amount)
            inst = op.emit(eng)
            if op.eng == "Q":
                inst.then_inc(sems[op.stream], op.amount)
            elif op.inc and op.custom_sem is None:
                assert inst is not None, f"op {op.idx} on {eng_name} needs inc"
                inst.then_inc(sems[op.eng], op.amount)


def build_program():
    nc = bass.Bass("TRN2", target_bir_lowering=False,
                   detect_race_conditions=False)

    d_x = nc.dram_tensor("x", [TOK, D], F32, kind="ExternalInput").ap()
    d_wk = nc.dram_tensor("wk", [D, D], F32, kind="ExternalInput").ap()
    d_wv = nc.dram_tensor("wv", [D, D], F32, kind="ExternalInput").ap()
    d_wr = nc.dram_tensor("wr", [D, D], F32, kind="ExternalInput").ap()
    d_wo = nc.dram_tensor("wo", [D, D], F32, kind="ExternalInput").ap()
    d_wsel = nc.dram_tensor("wsel", [8, D, D], F32, kind="ExternalInput").ap()
    d_bsel = nc.dram_tensor("bsel", [8, D], F32, kind="ExternalInput").ap()
    d_selw = nc.dram_tensor("selw", [P, 8], F32, kind="ExternalInput").ap()
    d_selwk = nc.dram_tensor("selwk", [8, 1], F32, kind="ExternalInput").ap()
    d_tdec = nc.dram_tensor("tdec", [P, 8], F32, kind="ExternalInput").ap()
    d_tfir = nc.dram_tensor("tfir", [P, 8], F32, kind="ExternalInput").ap()
    d_identb = nc.dram_tensor("identb", [P, P], BF16, kind="ExternalInput").ap()
    d_ones1 = nc.dram_tensor("ones1", [1, P], BF16, kind="ExternalInput").ap()
    d_hmask = nc.dram_tensor("hmask", [P, 1], F32, kind="ExternalInput").ap()
    d_xw = nc.dram_tensor("xw", [TOK, D], F32, kind="ExternalInput").ap()
    d_y = nc.dram_tensor("y", [TOK, D], F32, kind="ExternalOutput").ap()

    es = ExitStack()
    sb = lambda name, shape, dt: es.enter_context(nc.sbuf_tensor(name, shape, dt))

    x_buf = [sb(f"x_buf{i}", [P, 2048], F32) for i in range(3)]
    xn_buf = [sb(f"xn_buf{i}", [P, 2048], BF16) for i in range(2)]
    xnt_buf = [sb(f"xnt_buf{i}", [P, 2048], BF16) for i in range(2)]
    w_bf = {n: sb(f"w_{n}", [P, 8192], BF16) for n in
            ("wk", "wv", "wr", "wo", "wm")}
    wstage = [sb(f"wstage{i}", [P, 1024], F32) for i in range(4)]
    wselbf = [sb(f"wselbf{i}", [P, 1024], BF16) for i in range(2)]
    kv_buf = sb("kv_buf", [P, 2048], F32)
    y_buf = sb("y_buf", [P, 8 * 260], F32)
    r_sb = sb("r_sb", [P, 2048], BF16)
    outr = sb("outr", [P, 2048], BF16)
    wkvf = [sb(f"wkvf{i}", [P, TC], F32) for i in range(2)]
    sg_sb = [sb(f"sg_sb{i}", [P, 512], BF16) for i in range(2)]
    gt_sb = [sb(f"gt_sb{i}", [P, 512], F32) for i in range(4)]
    k_sb = [sb(f"k_sb{i}", [P, TC], F32) for i in range(2)]
    sqscr = sb("sqscr", [P, 1024], BF16)
    s_tdec = sb("s_tdec", [P, 8], F32)
    s_tfir = sb("s_tfir", [P, 8], F32)
    s_expd = sb("s_expd", [P, 8], F32)
    s_wneg = sb("s_wneg", [P, 8], F32)
    s_dec8 = sb("s_dec8", [P, 8], F32)
    s_eu8 = sb("s_eu8", [P, 8], F32)
    s_ident = sb("s_ident", [P, P], BF16)
    s_ones1 = sb("s_ones1", [1, P], BF16)
    s_selw = sb("s_selw", [P, 8], F32)
    s_selwk = sb("s_selwk", [8, 1], F32)
    s_selwkb = sb("s_selwkb", [8, 1], BF16)
    s_bselb = sb("s_bselb", [8, 1024], BF16)
    s_biasrow = sb("s_biasrow", [1, 1024], BF16)
    s_wik = sb("s_wik", [P, 8 * P], BF16)
    s_hmask = sb("s_hmask", [P, 1], F32)
    s_sum = sb("s_sum", [P, 2], F32)
    s_mean = sb("s_mean", [P, 2], F32)
    s_sumsq = sb("s_sumsq", [P, 2], F32)
    s_var = sb("s_var", [P, 2], F32)
    s_msq = sb("s_msq", [P, 2], F32)
    s_lnv = sb("s_lnv", [P, 2], F32)
    s_rstd = sb("s_rstd", [P, 4], F32)
    s_nmr = sb("s_nmr", [P, 4], F32)

    pf = [es.enter_context(nc.psum_tensor(f"pf{i}", [P, 512], F32)) for i in range(6)]
    pb = [es.enter_context(nc.psum_tensor(f"pbt{i}", [P, 1024], BF16)) for i in range(2)]

    sems = {}
    for name in (list(ENGS) + ["qs", "qx0", "qx1", "qx2", "qo0", "qo1"]
                 + [f"qw{i}" for i in range(4)]):
        sems[name] = es.enter_context(nc.semaphore(f"sem_{name}"))

    p = Prog()

    smalls = [
        (s_tdec, d_tdec), (s_tfir, d_tfir),
        (s_ident, d_identb), (s_ones1, d_ones1), (s_selw, d_selw),
        (s_selwk, d_selwk), (s_hmask, d_hmask),
    ]
    for dst, src in smalls:
        p.add("Q", (lambda dst, src: lambda e: e.dma_start(out=dst[:], in_=src))(dst, src),
              reads=(), writes=(f"sb:{dst.name}",), stream="qs")

    p.add("Q", lambda e: e.dma_start(out=wstage[0][0:8, :], in_=d_bsel),
          reads=("wstage0",), writes=("wstage0", "bselw"), stream="qs")
    p.add("V", lambda e: e.tensor_copy(s_bselb[:], wstage[0][0:8, :]),
          reads=("bselw",), writes=("sb:s_bselb", "wstage0"))

    p.add("S", lambda e: e.activation(s_expd[:], s_tdec[:], AF.Exp),
          reads=("sb:s_tdec",), writes=("sb:s_expd",))
    p.add("V", lambda e: e.tensor_scalar(out=s_wneg[:], in0=s_expd[:], scalar1=-1.0,
                                         scalar2=None, op0=ALU.mult),
          reads=("sb:s_expd",), writes=("sb:s_wneg",))
    p.add("S", lambda e: e.activation(s_dec8[:], s_wneg[:], AF.Exp),
          reads=("sb:s_wneg",), writes=("sb:s_dec8",))
    p.add("S", lambda e: e.activation(s_eu8[:], s_tfir[:], AF.Exp),
          reads=("sb:s_tfir",), writes=("sb:s_eu8",))
    p.add("V", lambda e: e.memset(y_buf[:], 0.0), reads=(), writes=("y_init",))

    p.add("V", lambda e: e.tensor_copy(s_selwkb[:], s_selwk[:]),
          reads=("sb:s_selwk",), writes=("sb:s_selwkb",))
    for oc in range(2):
        p.add("T", (lambda oc: lambda e: e.matmul(
            pf[0][0:1, :], s_selwkb[:], s_bselb[:, oc * 512:(oc + 1) * 512],
            start=True, stop=True))(oc),
              reads=("sb:s_selwkb", "sb:s_bselb"), writes=("pf0",))
        p.add("S", (lambda oc: lambda e: e.copy(
            s_biasrow[0:1, oc * 512:(oc + 1) * 512], pf[0][0:1, :]))(oc),
              reads=("pf0",), writes=("sb:s_biasrow",))
    for k in range(8):
        p.add("V", (lambda k: lambda e: e.tensor_scalar(
            out=s_wik[:, k * P:(k + 1) * P], in0=s_ident[:],
            scalar1=s_selw[:, k:k + 1], scalar2=None, op0=ALU.mult))(k),
              reads=("sb:s_ident", "sb:s_selw"), writes=(f"wik{k}",))

    def w_load(dram, dblk, slot):
        return lambda e: e.dma_start(out=wstage[slot][:],
                                     in_=dram[dblk * P:(dblk + 1) * P, :])

    def w_cast(name, dblk, slot):
        return lambda e: e.tensor_copy(
            w_bf[name][:, dblk * 1024:(dblk + 1) * 1024], wstage[slot][:])

    slot_ctr = [0]

    def stream_weight(name, dram):
        for dblk in range(DBLK):
            sl = slot_ctr[0] % 4
            slot_ctr[0] += 1
            p.add("Q", w_load(dram, dblk, sl), reads=(f"wstage{sl}",),
                  writes=(f"wstage{sl}w",), stream=f"qw{sl}")
            p.add("V", w_cast(name, dblk, sl), reads=(f"wstage{sl}w",),
                  writes=(f"wstage{sl}", f"wbf:{name}:{dblk}"))

    def x_load(c, tag):
        buf = x_buf[c % 2]
        return lambda e: e.dma_start(
            out=buf[:].rearrange("p (tb d) -> p tb d", tb=2),
            in_=d_x[c * TC:(c + 1) * TC, :].rearrange("(tb p) d -> p tb d", p=P))

    stream_weight("wk", d_wk)
    stream_weight("wv", d_wv)

    def stream_weight_dblk(name, dram, dblk):
        sl = slot_ctr[0] % 4
        slot_ctr[0] += 1
        p.add("Q", w_load(dram, dblk, sl), reads=(f"wstage{sl}",),
              writes=(f"wstage{sl}w",), stream=f"qw{sl}")
        p.add("V", w_cast(name, dblk, sl), reads=(f"wstage{sl}w",),
              writes=(f"wstage{sl}", f"wbf:{name}:{dblk}"))

    mix_jobs = []
    for dblk in range(DBLK):
        for k in range(8):
            mix_jobs.append((dblk, k))
    mix_ptr = [0]

    def emit_mix_jobs(n):
        for _ in range(n):
            if mix_ptr[0] >= len(mix_jobs):
                return
            dblk, k = mix_jobs[mix_ptr[0]]
            mix_ptr[0] += 1
            sl = slot_ctr[0] % 4
            slot_ctr[0] += 1
            p.add("Q", (lambda dblk, k, sl: lambda e: e.dma_start(
                out=wstage[sl][:],
                in_=d_wsel[k, dblk * P:(dblk + 1) * P, :]))(dblk, k, sl),
                  reads=(f"wstage{sl}",), writes=(f"wstage{sl}w",), stream=f"qw{sl}")
            wsl = mix_ptr[0] % 2
            p.add("V", (lambda sl, wsl: lambda e: e.tensor_copy(
                wselbf[wsl][:], wstage[sl][:]))(sl, wsl),
                  reads=(f"wstage{sl}w",), writes=(f"wstage{sl}", f"wselbf{wsl}"))
            for oc in range(2):
                bank = 4 + oc
                p.add("T", (lambda dblk, k, oc, bank, wsl: lambda e: e.matmul(
                    pf[bank][:], s_wik[:, k * P:(k + 1) * P],
                    wselbf[wsl][:, oc * 512:(oc + 1) * 512],
                    start=(k == 0), stop=(k == 7)))(dblk, k, oc, bank, wsl),
                      reads=(f"wik{k}", f"wselbf{wsl}"), writes=(f"pf{bank}",))
            if k == 7:
                for oc in range(2):
                    bank = 4 + oc
                    p.add("S", (lambda dblk, oc, bank: lambda e: e.copy(
                        w_bf["wm"][:, dblk * 1024 + oc * 512: dblk * 1024 + (oc + 1) * 512],
                        pf[bank][:]))(dblk, oc, bank),
                          reads=(f"pf{bank}",), writes=(f"wbf:wm:{dblk}",))

    def layernorm(c, phase):
        buf = x_buf[c % 3]
        xr = f"x{c % 3}"
        for tb in range(2):
            tile = buf[:, tb * 1024:(tb + 1) * 1024]
            p.add("V", (lambda tile, tb: lambda e: e.tensor_reduce(
                out=s_sum[:, tb:tb + 1], in_=tile, axis=AX.X, op=ALU.add))(tile, tb),
                  reads=(xr,), writes=("lnsum",))
            p.add("S", (lambda c, tile, tb: lambda e: e.activation(
                sqscr[:], tile, AF.Square,
                accum_out=s_sumsq[:, tb:tb + 1]))(c, tile, tb),
                  reads=(xr,), writes=("sqscr", "lnsumsq"))
        p.add("V", lambda e: e.tensor_scalar(out=s_mean[:], in0=s_sum[:],
                                             scalar1=1.0 / D, scalar2=None, op0=ALU.mult),
              reads=("lnsum",), writes=("lnmean",))
        p.add("V", lambda e: e.tensor_scalar(out=s_var[:], in0=s_sumsq[:],
                                             scalar1=1.0 / D, scalar2=EPS,
                                             op0=ALU.mult, op1=ALU.add),
              reads=("lnsumsq",), writes=("lnvar",))
        p.add("V", lambda e: e.tensor_tensor(out=s_msq[:], in0=s_mean[:],
                                             in1=s_mean[:], op=ALU.mult),
              reads=("lnmean",), writes=("lnmsq",))
        p.add("V", lambda e: e.tensor_tensor(out=s_var[:], in0=s_var[:],
                                             in1=s_msq[:], op=ALU.subtract),
              reads=("lnvar", "lnmsq"), writes=("lnvar",))
        p.add("S", lambda e: e.activation(s_lnv[:], s_var[:], AF.Ln),
              reads=("lnvar",), writes=("lnlnv",))
        pr = c % 2
        p.add("S", (lambda pr: lambda e: e.activation(
            s_rstd[:, 2 * pr:2 * pr + 2], s_lnv[:], AF.Exp, scale=-0.5))(pr),
              reads=("lnlnv",), writes=(f"rstdc{pr}",))
        p.add("V", (lambda pr: lambda e: e.scalar_tensor_tensor(
            out=s_nmr[:, 2 * pr:2 * pr + 2], in0=s_mean[:], scalar=-1.0,
            in1=s_rstd[:, 2 * pr:2 * pr + 2], op0=ALU.mult, op1=ALU.mult))(pr),
              reads=("lnmean", f"rstdc{pr}"), writes=(f"nmrc{pr}",))

    def normalize(c):
        buf = x_buf[c % 3]
        xr = f"x{c % 3}"
        for tb in range(2):
            pr = c % 2
            p.add("S", (lambda c, tb, buf, pr: lambda e: e.activation(
                xn_buf[c % 2][:, tb * 1024:(tb + 1) * 1024],
                buf[:, tb * 1024:(tb + 1) * 1024], AF.Identity,
                bias=s_nmr[:, 2 * pr + tb:2 * pr + tb + 1],
                scale=s_rstd[:, 2 * pr + tb:2 * pr + tb + 1]))(c, tb, buf, pr),
                  reads=(xr, f"rstdc{pr}", f"nmrc{pr}"),
                  writes=(f"xn{c % 2}t{tb}",))

    def transposes(c):
        par = c % 2
        for dblk in range(DBLK):
            bank = dblk % 2
            for tb in range(2):
                p.add("T", (lambda par, dblk, tb, bank: lambda e: e.transpose(
                    pb[bank][:, tb * P:(tb + 1) * P],
                    xn_buf[par][:, tb * 1024 + dblk * P: tb * 1024 + (dblk + 1) * P],
                    s_ident[:]))(par, dblk, tb, bank),
                      reads=(f"xn{par}t{tb}", "sb:s_ident"),
                      writes=(f"pbt{bank}",))
            p.add("S", (lambda par, dblk, bank: lambda e: e.copy(
                xnt_buf[par][:, dblk * TC:(dblk + 1) * TC],
                pb[bank][:, 0:TC]))(par, dblk, bank),
                  reads=(f"pbt{bank}",), writes=(f"xnt{par}d{dblk}",))

    NG = 2 * NCH

    def load_x(g):
        par = g % 3
        dram = d_xw if g < NCH else d_x
        c = g % NCH
        p.add("Q", (lambda dram, c, par: lambda e: e.dma_start(
            out=x_buf[par][:].rearrange("p (tb d) -> p tb d", tb=2),
            in_=dram[c * TC:(c + 1) * TC, :].rearrange("(tb p) d -> p tb d", p=P)))(dram, c, par),
              reads=(f"x{par}",), writes=(f"x{par}",), stream=f"qx{par}")

    def stats(g):
        if g >= NG:
            return
        if g > 1:
            load_x(g)
        layernorm(g, "m")

    def norm1(g):
        if g >= NG:
            return
        normalize(g)

    def kv_mms(g, warm):
        par = g % 2
        for eb in range(CB):
            kbank = 0 + (eb % 2)
            vbank = 2 + (eb % 2)
            for db in range(DBLK):
                p.add("T", (lambda par, eb, db, kbank: lambda e: e.matmul(
                    pf[kbank][:, 0:TC],
                    w_bf["wk"][:, db * 1024 + eb * P: db * 1024 + (eb + 1) * P],
                    xnt_buf[par][:, db * TC:(db + 1) * TC],
                    start=(db == 0), stop=(db == DBLK - 1)))(par, eb, db, kbank),
                      reads=(f"wbf:wk:{db}", f"xnt{par}d{db}"),
                      writes=(f"pf{kbank}",))
            for db in range(DBLK):
                p.add("T", (lambda par, eb, db, vbank: lambda e: e.matmul(
                    pf[vbank][:, 0:TC],
                    w_bf["wv"][:, db * 1024 + eb * P: db * 1024 + (eb + 1) * P],
                    xnt_buf[par][:, db * TC:(db + 1) * TC],
                    start=(db == 0), stop=(db == DBLK - 1)))(par, eb, db, vbank),
                      reads=(f"wbf:wv:{db}", f"xnt{par}d{db}"),
                      writes=(f"pf{vbank}",))
            ksl = eb % 2
            if warm:
                p.add("S", (lambda eb, kbank, ksl: lambda e: e.activation(
                    k_sb[ksl][:], pf[kbank][:, 0:TC], AF.Identity,
                    scale=s_hmask[:, 0:1]))(eb, kbank, ksl),
                      reads=(f"pf{kbank}", "sb:s_hmask"), writes=(f"ksb{ksl}",))
            else:
                p.add("S", (lambda eb, kbank, ksl: lambda e: e.copy(
                    k_sb[ksl][:], pf[kbank][:, 0:TC]))(eb, kbank, ksl),
                      reads=(f"pf{kbank}",), writes=(f"ksb{ksl}",))
            p.add("V", (lambda eb, vbank, ksl: lambda e: e.tensor_tensor(
                out=kv_buf[:, eb * TC:(eb + 1) * TC], in0=k_sb[ksl][:],
                in1=pf[vbank][:, 0:TC], op=ALU.mult))(eb, vbank, ksl),
                  reads=(f"ksb{ksl}", f"pf{vbank}"), writes=(f"kv{eb}",))

    def scans(g, with_wkv):
        for cb in range(CB):
            ybase = cb * 260
            if g > 0:
                p.add("V", (lambda cb, ybase: lambda e: e.tensor_copy(
                    y_buf[:, ybase:ybase + 1], y_buf[:, ybase + TC:ybase + TC + 1]))(cb, ybase),
                      reads=(f"y{cb}",), writes=(f"y{cb}c",))
            init = 0.0 if g == 0 else None
            p.add("V", (lambda cb, ybase, init: lambda e: e.tensor_tensor_scan(
                out=y_buf[:, ybase + 1:ybase + 1 + TC],
                data0=s_dec8[:, cb:cb + 1].broadcast_to((P, TC)),
                data1=kv_buf[:, cb * TC:(cb + 1) * TC],
                initial=(y_buf[:, ybase:ybase + 1] if init is None else 0.0),
                op0=ALU.mult, op1=ALU.add))(cb, ybase, init),
                  reads=("sb:s_dec8", f"kv{cb}", f"y{cb}c", "y_init"),
                  writes=(f"y{cb}",))
            if with_wkv:
                sl = cb % 2
                p.add("V", (lambda cb, sl: lambda e: e.tensor_scalar(
                    out=wkvf[sl][:], in0=kv_buf[:, cb * TC:(cb + 1) * TC],
                    scalar1=s_eu8[:, cb:cb + 1], scalar2=None, op0=ALU.mult))(cb, sl),
                      reads=(f"kv{cb}", "sb:s_eu8"), writes=(f"wkvf{sl}",))
                p.add("V", (lambda cb, ybase, sl: lambda e: e.tensor_tensor(
                    out=wkvf[sl][:], in0=wkvf[sl][:],
                    in1=y_buf[:, ybase:ybase + TC], op=ALU.add))(cb, ybase, sl),
                      reads=(f"wkvf{sl}", f"y{cb}", f"y{cb}c", "y_init"),
                      writes=(f"wkvf{sl}",))
                p.add("V", (lambda cb, sl: lambda e: e.tensor_tensor(
                    out=outr[:, cb * TC:(cb + 1) * TC], in0=wkvf[sl][:],
                    in1=r_sb[:, cb * TC:(cb + 1) * TC], op=ALU.mult))(cb, sl),
                      reads=(f"wkvf{sl}", "rsball"), writes=("outr",))

    def r_mms(g):
        par = g % 2
        for eb in range(CB):
            rbank = 0 + (eb % 2)
            for db in range(DBLK):
                p.add("T", (lambda par, eb, db, rbank: lambda e: e.matmul(
                    pf[rbank][:, 0:TC],
                    w_bf["wr"][:, db * 1024 + eb * P: db * 1024 + (eb + 1) * P],
                    xnt_buf[par][:, db * TC:(db + 1) * TC],
                    start=(db == 0), stop=(db == DBLK - 1)))(par, eb, db, rbank),
                      reads=(f"wbf:wr:{db}", f"xnt{par}d{db}"),
                      writes=(f"pf{rbank}",))
            p.add("S", (lambda eb, rbank: lambda e: e.activation(
                r_sb[:, eb * TC:(eb + 1) * TC], pf[rbank][:, 0:TC],
                AF.Sigmoid))(eb, rbank),
                  reads=(f"pf{rbank}",), writes=("rsball",))

    def quadrants_ffn(g):
        par = g % 2
        for q, (tb, oc) in enumerate([(0, 0), (0, 1), (1, 0), (1, 1)]):
            fbank = 4 + oc
            for db in range(DBLK):
                p.add("T", (lambda par, tb, oc, db, fbank: lambda e: e.matmul(
                    pf[fbank][:],
                    xnt_buf[par][:, db * TC + tb * P: db * TC + (tb + 1) * P],
                    w_bf["wm"][:, db * 1024 + oc * 512: db * 1024 + (oc + 1) * 512],
                    start=(db == 0), stop=False))(par, tb, oc, db, fbank),
                      reads=(f"xnt{par}d{db}", f"wbf:wm:{db}"),
                      writes=(f"pf{fbank}",))
            p.add("T", (lambda tb, oc, fbank: lambda e: e.matmul(
                pf[fbank][:], s_ones1[:],
                s_biasrow[0:1, oc * 512:(oc + 1) * 512],
                start=False, stop=True))(tb, oc, fbank),
                  reads=("sb:s_ones1", "sb:s_biasrow"),
                  writes=(f"pf{fbank}",))
            sl = oc
            p.add("S", (lambda fbank, sl: lambda e: e.activation(
                sg_sb[sl][:], pf[fbank][:], AF.Sigmoid))(fbank, sl),
                  reads=(f"pf{fbank}",), writes=(f"sg{sl}",))
            p.add("V", (lambda fbank, sl, q: lambda e: e.tensor_tensor(
                out=gt_sb[q][:], in0=pf[fbank][:], in1=sg_sb[sl][:],
                op=ALU.mult))(fbank, sl, q),
                  reads=(f"pf{fbank}", f"sg{sl}"), writes=(f"gt{q}",))
    def quadrants_wo(g):
        par = g % 2
        xpar = g % 3
        c = g - NCH
        for q, (tb, oc) in enumerate([(0, 0), (0, 1), (1, 0), (1, 1)]):
            bank = 2 + oc
            for eb in range(CB):
                p.add("T", (lambda par, tb, oc, eb, bank: lambda e: e.matmul(
                    pf[bank][:],
                    outr[:, eb * TC + tb * P: eb * TC + (tb + 1) * P],
                    w_bf["wo"][:, eb * 1024 + oc * 512: eb * 1024 + (oc + 1) * 512],
                    start=(eb == 0), stop=(eb == CB - 1)))(par, tb, oc, eb, bank),
                      reads=("outr", f"wbf:wo:{eb}"),
                      writes=(f"pf{bank}",))
            p.add("V", (lambda bank, q: lambda e: e.tensor_tensor(
                out=gt_sb[q][:], in0=pf[bank][:], in1=gt_sb[q][:],
                op=ALU.mult))(bank, q),
                  reads=(f"pf{bank}", f"gt{q}"), writes=(f"gt{q}",))
            p.add("V", (lambda xpar, tb, oc, q: lambda e: e.tensor_tensor(
                out=x_buf[xpar][:, tb * 1024 + oc * 512: tb * 1024 + (oc + 1) * 512],
                in0=gt_sb[q][:],
                in1=x_buf[xpar][:, tb * 1024 + oc * 512: tb * 1024 + (oc + 1) * 512],
                op=ALU.add))(xpar, tb, oc, q),
                  reads=(f"gt{q}", f"x{xpar}"), writes=(f"x{xpar}",))
        p.add("Q", (lambda c, par, xpar: lambda e: e.dma_start(
            out=d_y[c * TC:(c + 1) * TC, :].rearrange("(tb p) d -> p tb d", p=P),
            in_=x_buf[xpar][:].rearrange("p (tb d) -> p tb d", tb=2)))(c, par, xpar),
              reads=(f"x{xpar}",), writes=(f"out{c}",), stream=f"qo{par}")

    def back_warm(g):
        transposes(g)
        kv_mms(g, True)
        stream_weight_dblk("wr", d_wr, g)
        stream_weight_dblk("wo", d_wo, g)
        emit_mix_jobs(8)
        scans(g, with_wkv=False)

    def head_main(g):
        transposes(g)
        kv_mms(g, False)
        r_mms(g)

    load_x(0)
    load_x(1)
    stats(0)
    norm1(0)
    stats(1)
    norm1(1)
    for g in range(NCH):
        stats(g + 2)
        norm1(g + 1) if g + 1 >= 2 else None
        back_warm(g)
    head_main(NCH)
    for g in range(NCH, NG):
        stats(g + 2)
        quadrants_ffn(g)
        norm1(g + 1)
        scans(g, with_wkv=True)
        if g + 1 < NG:
            head_main(g + 1)
        quadrants_wo(g)

    p.add("G", lambda e: e.engine_nop(),
          reads=tuple(f"out{c}" for c in range(NCH)), writes=())

    p.finalize()

    with nc.Block() as block:
        @block.tensor
        def _(eng):
            p.emit_engine("T", eng, sems)

        @block.vector
        def _(eng):
            p.emit_engine("V", eng, sems)

        @block.scalar
        def _(eng):
            p.emit_engine("S", eng, sems)

        @block.sync
        def _(eng):
            p.emit_engine("Q", eng, sems)

        @block.gpsimd
        def _(eng):
            p.emit_engine("G", eng, sems)

    es.close()
    return nc


_NC_CACHE = None


def kernel(L0_attn_x, L0_attn_Wk, L0_attn_Wv, L0_attn_Wr, L0_attn_Wo,
           L0_attn_time_decay, L0_attn_time_first,
           L0_ffn_W, L0_ffn_b, L0_ffn_ids, L0_ffn_weights):
    global _NC_CACHE
    x = np.asarray(L0_attn_x, np.float32)
    wkT = np.ascontiguousarray(np.asarray(L0_attn_Wk, np.float32).T)
    wvT = np.ascontiguousarray(np.asarray(L0_attn_Wv, np.float32).T)
    wrT = np.ascontiguousarray(np.asarray(L0_attn_Wr, np.float32).T)
    woT = np.ascontiguousarray(np.asarray(L0_attn_Wo, np.float32).T)
    ids = np.asarray(L0_ffn_ids).astype(np.int64)
    wselT = np.ascontiguousarray(
        np.asarray(L0_ffn_W, np.float32)[ids].transpose(0, 2, 1))
    bsel = np.ascontiguousarray(np.asarray(L0_ffn_b, np.float32)[ids])
    selw = np.ascontiguousarray(
        np.broadcast_to(np.asarray(L0_ffn_weights, np.float32), (P, 8)))
    selwk = np.ascontiguousarray(
        np.asarray(L0_ffn_weights, np.float32).reshape(8, 1))
    tdec = np.ascontiguousarray(
        np.asarray(L0_attn_time_decay, np.float32).reshape(8, P).T)
    tfir = np.ascontiguousarray(
        np.asarray(L0_attn_time_first, np.float32).reshape(8, P).T)
    identb = np.eye(P, dtype=np.float32).astype(ml_dtypes.bfloat16)
    ones1 = np.ones((1, P), dtype=ml_dtypes.bfloat16)

    if _NC_CACHE is None:
        _NC_CACHE = build_program()
    nc = _NC_CACHE

    in_maps = []
    for c in range(NCORE):
        b, h = c // 2, c % 2
        in_maps.append({
            "x": np.ascontiguousarray(x[b, h * TOK:(h + 1) * TOK, :]),
            "xw": np.ascontiguousarray(x[b, 0:TOK, :]),
            "wk": wkT, "wv": wvT, "wr": wrT, "wo": woT,
            "wsel": wselT, "bsel": bsel, "selw": selw, "selwk": selwk,
            "tdec": tdec, "tfir": tfir,
            "identb": identb, "ones1": ones1,
            "hmask": np.full((P, 1), float(h), np.float32),
        })

    res = run_bass_kernel_spmd(nc, in_maps, core_ids=list(range(NCORE)))
    out = np.empty((4, 4096, D), np.float32)
    for c in range(NCORE):
        b, h = c // 2, c % 2
        out[b, h * TOK:(h + 1) * TOK, :] = np.asarray(res.results[c]["y"])
    return out


# version 28
# speedup vs baseline: 1.0000x; 1.0000x over previous
import numpy as np
import ml_dtypes
from contextlib import ExitStack

import concourse.bass as bass
import concourse.mybir as mybir
from concourse.bass_utils import run_bass_kernel_spmd

F32 = mybir.dt.float32
BF16 = mybir.dt.bfloat16
U32 = mybir.dt.uint32
AF = mybir.ActivationFunctionType
ALU = mybir.AluOpType
AX = mybir.AxisListType

NCORE = 8
P = 128
D = 1024
TOK = 2048
TC = 256
NCH = TOK // TC
CB = 8
DBLK = 8
EPS = 1e-5

ENGS = ("T", "V", "S", "Q", "G")

_DETECTOR_SAFE = False


class _Op:
    __slots__ = ("eng", "emit", "reads", "writes", "deps", "inc", "idx",
                 "val", "waits", "custom_sem", "amount", "stream", "issue_val")

    def __init__(self, eng, emit, reads, writes, custom_sem=None):
        self.eng = eng
        self.stream = None
        self.emit = emit
        self.reads = tuple(reads)
        self.writes = tuple(writes)
        self.custom_sem = custom_sem
        self.deps = None
        self.inc = False
        self.val = None
        self.waits = None
        self.amount = 16 if eng in ("Q",) else 1


class Prog:
    def __init__(self):
        self.ops = []

    def add(self, eng, emit, reads=(), writes=(), custom_sem=None, amount=None,
            stream=None):
        op = _Op(eng, emit, reads, writes, custom_sem)
        if amount is not None:
            op.amount = amount
        if eng == "Q":
            op.stream = stream or "qs"
        op.idx = len(self.ops)
        self.ops.append(op)
        return op

    def finalize(self):
        last_writer = {}
        readers = {}
        for op in self.ops:
            deps = set()
            for r in op.reads:
                w = last_writer.get(r)
                if w is not None and w is not op:
                    deps.add(w)
            for w in op.writes:
                pw = last_writer.get(w)
                if pw is not None and pw is not op:
                    deps.add(pw)
                for rd in readers.get(w, ()):
                    if rd is not op:
                        deps.add(rd)
            for r in op.reads:
                readers.setdefault(r, []).append(op)
            for w in op.writes:
                last_writer[w] = op
                readers[w] = []
            op.deps = [d for d in deps
                       if not (d.eng == op.eng and op.eng == "T")
                       and not (op.eng == "Q" and d.eng == "Q"
                                and d.stream == op.stream)]
        for op in self.ops:
            for d in op.deps:
                if d.custom_sem is None:
                    d.inc = True
        counters = {e: 0 for e in ENGS}
        qcounters = {}
        for op in self.ops:
            if op.eng == "Q":
                op.inc = True
                qcounters[op.stream] = qcounters.get(op.stream, 0) + op.amount
                op.val = qcounters[op.stream]
                op.issue_val = op.val
            elif op.custom_sem is not None:
                op.val = None
            elif op.inc:
                counters[op.eng] += op.amount
                op.val = counters[op.eng]
            else:
                op.val = None
        qs_total = qcounters.get("qs", 0)
        for op in self.ops:
            if op.eng == "Q" and op.stream == "qs":
                op.val = qs_total
        marks = {e: {f: 0 for f in ENGS} for e in ENGS}
        cmarks = {e: {} for e in ENGS}
        for op in self.ops:
            waits = []
            need = {}
            cneed = {}
            for d in op.deps:
                if d.custom_sem is not None:
                    nm, v = d.custom_sem
                    cneed[nm] = max(cneed.get(nm, 0), v)
                elif d.eng == "Q":
                    cneed[d.stream] = max(cneed.get(d.stream, 0), d.val)
                else:
                    assert d.val is not None
                    need[d.eng] = max(need.get(d.eng, 0), d.val)
            for e, v in sorted(need.items()):
                if v > marks[op.eng][e]:
                    waits.append((e, v))
                    marks[op.eng][e] = v
            for nm, v in sorted(cneed.items()):
                if v > cmarks[op.eng].get(nm, 0):
                    waits.append((nm, v))
                    cmarks[op.eng][nm] = v
            op.waits = waits

    def emit_engine(self, eng_name, eng, sems):
        for op in self.ops:
            if op.eng != eng_name:
                continue
            for tgt, v in op.waits:
                eng.wait_ge(sems[tgt], v)
            if _DETECTOR_SAFE and op.eng == "Q" and op.issue_val > op.amount:
                eng.wait_ge(sems[op.stream], op.issue_val - op.amount)
            inst = op.emit(eng)
            if op.eng == "Q":
                inst.then_inc(sems[op.stream], op.amount)
            elif op.inc and op.custom_sem is None:
                assert inst is not None, f"op {op.idx} on {eng_name} needs inc"
                inst.then_inc(sems[op.eng], op.amount)


def build_program():
    nc = bass.Bass("TRN2", target_bir_lowering=False,
                   detect_race_conditions=False)

    d_x = nc.dram_tensor("x", [TOK, D], F32, kind="ExternalInput").ap()
    d_wk = nc.dram_tensor("wk", [D, D], F32, kind="ExternalInput").ap()
    d_wv = nc.dram_tensor("wv", [D, D], F32, kind="ExternalInput").ap()
    d_wr = nc.dram_tensor("wr", [D, D], F32, kind="ExternalInput").ap()
    d_wo = nc.dram_tensor("wo", [D, D], F32, kind="ExternalInput").ap()
    d_wsel = nc.dram_tensor("wsel", [8, D, D], F32, kind="ExternalInput").ap()
    d_bsel = nc.dram_tensor("bsel", [8, D], F32, kind="ExternalInput").ap()
    d_selw = nc.dram_tensor("selw", [P, 8], F32, kind="ExternalInput").ap()
    d_selwk = nc.dram_tensor("selwk", [8, 1], F32, kind="ExternalInput").ap()
    d_tdec = nc.dram_tensor("tdec", [P, 8], F32, kind="ExternalInput").ap()
    d_tfir = nc.dram_tensor("tfir", [P, 8], F32, kind="ExternalInput").ap()
    d_identb = nc.dram_tensor("identb", [P, P], BF16, kind="ExternalInput").ap()
    d_ones1 = nc.dram_tensor("ones1", [1, P], BF16, kind="ExternalInput").ap()
    d_hmask = nc.dram_tensor("hmask", [P, 1], F32, kind="ExternalInput").ap()
    d_xw = nc.dram_tensor("xw", [TOK, D], F32, kind="ExternalInput").ap()
    d_y = nc.dram_tensor("y", [TOK, D], F32, kind="ExternalOutput").ap()

    es = ExitStack()
    sb = lambda name, shape, dt: es.enter_context(nc.sbuf_tensor(name, shape, dt))

    x_buf = [sb(f"x_buf{i}", [P, 2048], F32) for i in range(3)]
    xn_buf = [sb(f"xn_buf{i}", [P, 2048], BF16) for i in range(2)]
    xnt_buf = [sb(f"xnt_buf{i}", [P, 2048], BF16) for i in range(2)]
    w_bf = {n: sb(f"w_{n}", [P, 8192], BF16) for n in
            ("wk", "wv", "wr", "wo", "wm")}
    wstage = [sb(f"wstage{i}", [P, 1024], F32) for i in range(4)]
    wselbf = [sb(f"wselbf{i}", [P, 1024], BF16) for i in range(2)]
    kv_buf = sb("kv_buf", [P, 2048], F32)
    y_buf = sb("y_buf", [P, 8 * 260], F32)
    r_sb = sb("r_sb", [P, 2048], BF16)
    outr = sb("outr", [P, 2048], BF16)
    wkvf = [sb(f"wkvf{i}", [P, TC], F32) for i in range(2)]
    sg_sb = [sb(f"sg_sb{i}", [P, 512], BF16) for i in range(2)]
    gt_sb = [sb(f"gt_sb{i}", [P, 512], F32) for i in range(4)]
    k_sb = [sb(f"k_sb{i}", [P, TC], F32) for i in range(2)]
    sqscr = sb("sqscr", [P, 1024], BF16)
    s_tdec = sb("s_tdec", [P, 8], F32)
    s_tfir = sb("s_tfir", [P, 8], F32)
    s_expd = sb("s_expd", [P, 8], F32)
    s_wneg = sb("s_wneg", [P, 8], F32)
    s_dec8 = sb("s_dec8", [P, 8], F32)
    s_eu8 = sb("s_eu8", [P, 8], F32)
    s_ident = sb("s_ident", [P, P], BF16)
    s_ones1 = sb("s_ones1", [1, P], BF16)
    s_selw = sb("s_selw", [P, 8], F32)
    s_selwk = sb("s_selwk", [8, 1], F32)
    s_selwkb = sb("s_selwkb", [8, 1], BF16)
    s_bselb = sb("s_bselb", [8, 1024], BF16)
    s_biasrow = sb("s_biasrow", [1, 1024], BF16)
    s_wik = sb("s_wik", [P, 8 * P], BF16)
    s_hmask = sb("s_hmask", [P, 1], F32)
    s_sum = sb("s_sum", [P, 2], F32)
    s_mean = sb("s_mean", [P, 2], F32)
    s_sumsq = sb("s_sumsq", [P, 2], F32)
    s_var = sb("s_var", [P, 2], F32)
    s_msq = sb("s_msq", [P, 2], F32)
    s_lnv = sb("s_lnv", [P, 2], F32)
    s_rstd = sb("s_rstd", [P, 4], F32)
    s_nmr = sb("s_nmr", [P, 4], F32)

    pf = [es.enter_context(nc.psum_tensor(f"pf{i}", [P, 512], F32)) for i in range(6)]
    pb = [es.enter_context(nc.psum_tensor(f"pbt{i}", [P, 1024], BF16)) for i in range(2)]

    sems = {}
    for name in (list(ENGS) + ["qs", "qx0", "qx1", "qx2", "qo0", "qo1"]
                 + [f"qw{i}" for i in range(4)]):
        sems[name] = es.enter_context(nc.semaphore(f"sem_{name}"))

    p = Prog()

    smalls = [
        (s_tdec, d_tdec), (s_tfir, d_tfir),
        (s_ident, d_identb), (s_ones1, d_ones1), (s_selw, d_selw),
        (s_selwk, d_selwk), (s_hmask, d_hmask),
    ]
    for dst, src in smalls:
        p.add("Q", (lambda dst, src: lambda e: e.dma_start(out=dst[:], in_=src))(dst, src),
              reads=(), writes=(f"sb:{dst.name}",), stream="qs")

    p.add("Q", lambda e: e.dma_start(out=wstage[0][0:8, :], in_=d_bsel),
          reads=("wstage0",), writes=("wstage0", "bselw"), stream="qs")
    p.add("V", lambda e: e.tensor_copy(s_bselb[:], wstage[0][0:8, :]),
          reads=("bselw",), writes=("sb:s_bselb", "wstage0"))

    p.add("S", lambda e: e.activation(s_expd[:], s_tdec[:], AF.Exp),
          reads=("sb:s_tdec",), writes=("sb:s_expd",))
    p.add("V", lambda e: e.tensor_scalar(out=s_wneg[:], in0=s_expd[:], scalar1=-1.0,
                                         scalar2=None, op0=ALU.mult),
          reads=("sb:s_expd",), writes=("sb:s_wneg",))
    p.add("S", lambda e: e.activation(s_dec8[:], s_wneg[:], AF.Exp),
          reads=("sb:s_wneg",), writes=("sb:s_dec8",))
    p.add("S", lambda e: e.activation(s_eu8[:], s_tfir[:], AF.Exp),
          reads=("sb:s_tfir",), writes=("sb:s_eu8",))
    p.add("V", lambda e: e.memset(y_buf[:], 0.0), reads=(), writes=("y_init",))

    p.add("V", lambda e: e.tensor_copy(s_selwkb[:], s_selwk[:]),
          reads=("sb:s_selwk",), writes=("sb:s_selwkb",))
    for oc in range(2):
        p.add("T", (lambda oc: lambda e: e.matmul(
            pf[0][0:1, :], s_selwkb[:], s_bselb[:, oc * 512:(oc + 1) * 512],
            start=True, stop=True))(oc),
              reads=("sb:s_selwkb", "sb:s_bselb"), writes=("pf0",))
        p.add("S", (lambda oc: lambda e: e.copy(
            s_biasrow[0:1, oc * 512:(oc + 1) * 512], pf[0][0:1, :]))(oc),
              reads=("pf0",), writes=("sb:s_biasrow",))
    for k in range(8):
        p.add("V", (lambda k: lambda e: e.tensor_scalar(
            out=s_wik[:, k * P:(k + 1) * P], in0=s_ident[:],
            scalar1=s_selw[:, k:k + 1], scalar2=None, op0=ALU.mult))(k),
              reads=("sb:s_ident", "sb:s_selw"), writes=(f"wik{k}",))

    def w_load(dram, dblk, slot):
        return lambda e: e.dma_start(out=wstage[slot][:],
                                     in_=dram[dblk * P:(dblk + 1) * P, :])

    def w_cast(name, dblk, slot):
        return lambda e: e.tensor_copy(
            w_bf[name][:, dblk * 1024:(dblk + 1) * 1024], wstage[slot][:])

    slot_ctr = [0]

    def stream_weight(name, dram):
        for dblk in range(DBLK):
            sl = slot_ctr[0] % 4
            slot_ctr[0] += 1
            p.add("Q", w_load(dram, dblk, sl), reads=(f"wstage{sl}",),
                  writes=(f"wstage{sl}w",), stream=f"qw{sl}")
            p.add("V", w_cast(name, dblk, sl), reads=(f"wstage{sl}w",),
                  writes=(f"wstage{sl}", f"wbf:{name}:{dblk}"))

    def x_load(c, tag):
        buf = x_buf[c % 2]
        return lambda e: e.dma_start(
            out=buf[:].rearrange("p (tb d) -> p tb d", tb=2),
            in_=d_x[c * TC:(c + 1) * TC, :].rearrange("(tb p) d -> p tb d", p=P))

    stream_weight("wk", d_wk)
    stream_weight("wv", d_wv)

    def stream_weight_dblk(name, dram, dblk):
        sl = slot_ctr[0] % 4
        slot_ctr[0] += 1
        p.add("Q", w_load(dram, dblk, sl), reads=(f"wstage{sl}",),
              writes=(f"wstage{sl}w",), stream=f"qw{sl}")
        p.add("V", w_cast(name, dblk, sl), reads=(f"wstage{sl}w",),
              writes=(f"wstage{sl}", f"wbf:{name}:{dblk}"))

    mix_jobs = []
    for dblk in range(DBLK):
        for k in range(8):
            mix_jobs.append((dblk, k))
    mix_ptr = [0]

    def emit_mix_jobs(n):
        for _ in range(n):
            if mix_ptr[0] >= len(mix_jobs):
                return
            dblk, k = mix_jobs[mix_ptr[0]]
            mix_ptr[0] += 1
            sl = slot_ctr[0] % 4
            slot_ctr[0] += 1
            p.add("Q", (lambda dblk, k, sl: lambda e: e.dma_start(
                out=wstage[sl][:],
                in_=d_wsel[k, dblk * P:(dblk + 1) * P, :]))(dblk, k, sl),
                  reads=(f"wstage{sl}",), writes=(f"wstage{sl}w",), stream=f"qw{sl}")
            wsl = mix_ptr[0] % 2
            if mix_ptr[0] % 2 == 0:
                p.add("V", (lambda sl, wsl: lambda e: e.tensor_copy(
                    wselbf[wsl][:], wstage[sl][:]))(sl, wsl),
                      reads=(f"wstage{sl}w",), writes=(f"wstage{sl}", f"wselbf{wsl}"))
            else:
                p.add("S", (lambda sl, wsl: lambda e: e.copy(
                    wselbf[wsl][:], wstage[sl][:]))(sl, wsl),
                      reads=(f"wstage{sl}w",), writes=(f"wstage{sl}", f"wselbf{wsl}"))
            for oc in range(2):
                bank = 4 + oc
                p.add("T", (lambda dblk, k, oc, bank, wsl: lambda e: e.matmul(
                    pf[bank][:], s_wik[:, k * P:(k + 1) * P],
                    wselbf[wsl][:, oc * 512:(oc + 1) * 512],
                    start=(k == 0), stop=(k == 7)))(dblk, k, oc, bank, wsl),
                      reads=(f"wik{k}", f"wselbf{wsl}"), writes=(f"pf{bank}",))
            if k == 7:
                for oc in range(2):
                    bank = 4 + oc
                    p.add("S", (lambda dblk, oc, bank: lambda e: e.copy(
                        w_bf["wm"][:, dblk * 1024 + oc * 512: dblk * 1024 + (oc + 1) * 512],
                        pf[bank][:]))(dblk, oc, bank),
                          reads=(f"pf{bank}",), writes=(f"wbf:wm:{dblk}",))

    def layernorm(c, phase):
        buf = x_buf[c % 3]
        xr = f"x{c % 3}"
        for tb in range(2):
            tile = buf[:, tb * 1024:(tb + 1) * 1024]
            p.add("V", (lambda tile, tb: lambda e: e.tensor_reduce(
                out=s_sum[:, tb:tb + 1], in_=tile, axis=AX.X, op=ALU.add))(tile, tb),
                  reads=(xr,), writes=("lnsum",))
            p.add("S", (lambda c, tile, tb: lambda e: e.activation(
                sqscr[:], tile, AF.Square,
                accum_out=s_sumsq[:, tb:tb + 1]))(c, tile, tb),
                  reads=(xr,), writes=("sqscr", "lnsumsq"))
        p.add("V", lambda e: e.tensor_scalar(out=s_mean[:], in0=s_sum[:],
                                             scalar1=1.0 / D, scalar2=None, op0=ALU.mult),
              reads=("lnsum",), writes=("lnmean",))
        p.add("V", lambda e: e.tensor_scalar(out=s_var[:], in0=s_sumsq[:],
                                             scalar1=1.0 / D, scalar2=EPS,
                                             op0=ALU.mult, op1=ALU.add),
              reads=("lnsumsq",), writes=("lnvar",))
        p.add("V", lambda e: e.tensor_tensor(out=s_msq[:], in0=s_mean[:],
                                             in1=s_mean[:], op=ALU.mult),
              reads=("lnmean",), writes=("lnmsq",))
        p.add("V", lambda e: e.tensor_tensor(out=s_var[:], in0=s_var[:],
                                             in1=s_msq[:], op=ALU.subtract),
              reads=("lnvar", "lnmsq"), writes=("lnvar",))
        p.add("S", lambda e: e.activation(s_lnv[:], s_var[:], AF.Ln),
              reads=("lnvar",), writes=("lnlnv",))
        pr = c % 2
        p.add("S", (lambda pr: lambda e: e.activation(
            s_rstd[:, 2 * pr:2 * pr + 2], s_lnv[:], AF.Exp, scale=-0.5))(pr),
              reads=("lnlnv",), writes=(f"rstdc{pr}",))
        p.add("V", (lambda pr: lambda e: e.scalar_tensor_tensor(
            out=s_nmr[:, 2 * pr:2 * pr + 2], in0=s_mean[:], scalar=-1.0,
            in1=s_rstd[:, 2 * pr:2 * pr + 2], op0=ALU.mult, op1=ALU.mult))(pr),
              reads=("lnmean", f"rstdc{pr}"), writes=(f"nmrc{pr}",))

    def normalize(c):
        buf = x_buf[c % 3]
        xr = f"x{c % 3}"
        for tb in range(2):
            pr = c % 2
            p.add("S", (lambda c, tb, buf, pr: lambda e: e.activation(
                xn_buf[c % 2][:, tb * 1024:(tb + 1) * 1024],
                buf[:, tb * 1024:(tb + 1) * 1024], AF.Identity,
                bias=s_nmr[:, 2 * pr + tb:2 * pr + tb + 1],
                scale=s_rstd[:, 2 * pr + tb:2 * pr + tb + 1]))(c, tb, buf, pr),
                  reads=(xr, f"rstdc{pr}", f"nmrc{pr}"),
                  writes=(f"xn{c % 2}t{tb}",))

    def transposes(c):
        par = c % 2
        for dblk in range(DBLK):
            bank = dblk % 2
            for tb in range(2):
                p.add("T", (lambda par, dblk, tb, bank: lambda e: e.transpose(
                    pb[bank][:, tb * P:(tb + 1) * P],
                    xn_buf[par][:, tb * 1024 + dblk * P: tb * 1024 + (dblk + 1) * P],
                    s_ident[:]))(par, dblk, tb, bank),
                      reads=(f"xn{par}t{tb}", "sb:s_ident"),
                      writes=(f"pbt{bank}",))
            p.add("S", (lambda par, dblk, bank: lambda e: e.copy(
                xnt_buf[par][:, dblk * TC:(dblk + 1) * TC],
                pb[bank][:, 0:TC]))(par, dblk, bank),
                  reads=(f"pbt{bank}",), writes=(f"xnt{par}d{dblk}",))

    NG = 2 * NCH

    def load_x(g):
        par = g % 3
        dram = d_xw if g < NCH else d_x
        c = g % NCH
        p.add("Q", (lambda dram, c, par: lambda e: e.dma_start(
            out=x_buf[par][:].rearrange("p (tb d) -> p tb d", tb=2),
            in_=dram[c * TC:(c + 1) * TC, :].rearrange("(tb p) d -> p tb d", p=P)))(dram, c, par),
              reads=(f"x{par}",), writes=(f"x{par}",), stream=f"qx{par}")

    def stats(g):
        if g >= NG:
            return
        if g > 1:
            load_x(g)
        layernorm(g, "m")

    def norm1(g):
        if g >= NG:
            return
        normalize(g)

    def kv_mms(g, warm):
        par = g % 2
        for eb in range(CB):
            kbank = 0 + (eb % 2)
            vbank = 2 + (eb % 2)
            for db in range(DBLK):
                p.add("T", (lambda par, eb, db, kbank: lambda e: e.matmul(
                    pf[kbank][:, 0:TC],
                    w_bf["wk"][:, db * 1024 + eb * P: db * 1024 + (eb + 1) * P],
                    xnt_buf[par][:, db * TC:(db + 1) * TC],
                    start=(db == 0), stop=(db == DBLK - 1)))(par, eb, db, kbank),
                      reads=(f"wbf:wk:{db}", f"xnt{par}d{db}"),
                      writes=(f"pf{kbank}",))
            for db in range(DBLK):
                p.add("T", (lambda par, eb, db, vbank: lambda e: e.matmul(
                    pf[vbank][:, 0:TC],
                    w_bf["wv"][:, db * 1024 + eb * P: db * 1024 + (eb + 1) * P],
                    xnt_buf[par][:, db * TC:(db + 1) * TC],
                    start=(db == 0), stop=(db == DBLK - 1)))(par, eb, db, vbank),
                      reads=(f"wbf:wv:{db}", f"xnt{par}d{db}"),
                      writes=(f"pf{vbank}",))
            ksl = eb % 2
            if warm:
                p.add("S", (lambda eb, kbank, ksl: lambda e: e.activation(
                    k_sb[ksl][:], pf[kbank][:, 0:TC], AF.Identity,
                    scale=s_hmask[:, 0:1]))(eb, kbank, ksl),
                      reads=(f"pf{kbank}", "sb:s_hmask"), writes=(f"ksb{ksl}",))
            else:
                p.add("S", (lambda eb, kbank, ksl: lambda e: e.copy(
                    k_sb[ksl][:], pf[kbank][:, 0:TC]))(eb, kbank, ksl),
                      reads=(f"pf{kbank}",), writes=(f"ksb{ksl}",))
            p.add("V", (lambda eb, vbank, ksl: lambda e: e.tensor_tensor(
                out=kv_buf[:, eb * TC:(eb + 1) * TC], in0=k_sb[ksl][:],
                in1=pf[vbank][:, 0:TC], op=ALU.mult))(eb, vbank, ksl),
                  reads=(f"ksb{ksl}", f"pf{vbank}"), writes=(f"kv{eb}",))

    def scans(g, with_wkv):
        for cb in range(CB):
            ybase = cb * 260
            if g > 0:
                p.add("V", (lambda cb, ybase: lambda e: e.tensor_copy(
                    y_buf[:, ybase:ybase + 1], y_buf[:, ybase + TC:ybase + TC + 1]))(cb, ybase),
                      reads=(f"y{cb}",), writes=(f"y{cb}c",))
            init = 0.0 if g == 0 else None
            p.add("V", (lambda cb, ybase, init: lambda e: e.tensor_tensor_scan(
                out=y_buf[:, ybase + 1:ybase + 1 + TC],
                data0=s_dec8[:, cb:cb + 1].broadcast_to((P, TC)),
                data1=kv_buf[:, cb * TC:(cb + 1) * TC],
                initial=(y_buf[:, ybase:ybase + 1] if init is None else 0.0),
                op0=ALU.mult, op1=ALU.add))(cb, ybase, init),
                  reads=("sb:s_dec8", f"kv{cb}", f"y{cb}c", "y_init"),
                  writes=(f"y{cb}",))
            if with_wkv:
                sl = cb % 2
                p.add("V", (lambda cb, sl: lambda e: e.tensor_scalar(
                    out=wkvf[sl][:], in0=kv_buf[:, cb * TC:(cb + 1) * TC],
                    scalar1=s_eu8[:, cb:cb + 1], scalar2=None, op0=ALU.mult))(cb, sl),
                      reads=(f"kv{cb}", "sb:s_eu8"), writes=(f"wkvf{sl}",))
                p.add("V", (lambda cb, ybase, sl: lambda e: e.tensor_tensor(
                    out=wkvf[sl][:], in0=wkvf[sl][:],
                    in1=y_buf[:, ybase:ybase + TC], op=ALU.add))(cb, ybase, sl),
                      reads=(f"wkvf{sl}", f"y{cb}", f"y{cb}c", "y_init"),
                      writes=(f"wkvf{sl}",))
                p.add("V", (lambda cb, sl: lambda e: e.tensor_tensor(
                    out=outr[:, cb * TC:(cb + 1) * TC], in0=wkvf[sl][:],
                    in1=r_sb[:, cb * TC:(cb + 1) * TC], op=ALU.mult))(cb, sl),
                      reads=(f"wkvf{sl}", "rsball"), writes=("outr",))

    def r_mms(g):
        par = g % 2
        for eb in range(CB):
            rbank = 0 + (eb % 2)
            for db in range(DBLK):
                p.add("T", (lambda par, eb, db, rbank: lambda e: e.matmul(
                    pf[rbank][:, 0:TC],
                    w_bf["wr"][:, db * 1024 + eb * P: db * 1024 + (eb + 1) * P],
                    xnt_buf[par][:, db * TC:(db + 1) * TC],
                    start=(db == 0), stop=(db == DBLK - 1)))(par, eb, db, rbank),
                      reads=(f"wbf:wr:{db}", f"xnt{par}d{db}"),
                      writes=(f"pf{rbank}",))
            p.add("S", (lambda eb, rbank: lambda e: e.activation(
                r_sb[:, eb * TC:(eb + 1) * TC], pf[rbank][:, 0:TC],
                AF.Sigmoid))(eb, rbank),
                  reads=(f"pf{rbank}",), writes=("rsball",))

    def quadrants_ffn(g):
        par = g % 2
        for q, (tb, oc) in enumerate([(0, 0), (0, 1), (1, 0), (1, 1)]):
            fbank = 4 + oc
            for db in range(DBLK):
                p.add("T", (lambda par, tb, oc, db, fbank: lambda e: e.matmul(
                    pf[fbank][:],
                    xnt_buf[par][:, db * TC + tb * P: db * TC + (tb + 1) * P],
                    w_bf["wm"][:, db * 1024 + oc * 512: db * 1024 + (oc + 1) * 512],
                    start=(db == 0), stop=False))(par, tb, oc, db, fbank),
                      reads=(f"xnt{par}d{db}", f"wbf:wm:{db}"),
                      writes=(f"pf{fbank}",))
            p.add("T", (lambda tb, oc, fbank: lambda e: e.matmul(
                pf[fbank][:], s_ones1[:],
                s_biasrow[0:1, oc * 512:(oc + 1) * 512],
                start=False, stop=True))(tb, oc, fbank),
                  reads=("sb:s_ones1", "sb:s_biasrow"),
                  writes=(f"pf{fbank}",))
            sl = oc
            p.add("S", (lambda fbank, sl: lambda e: e.activation(
                sg_sb[sl][:], pf[fbank][:], AF.Sigmoid))(fbank, sl),
                  reads=(f"pf{fbank}",), writes=(f"sg{sl}",))
            p.add("V", (lambda fbank, sl, q: lambda e: e.tensor_tensor(
                out=gt_sb[q][:], in0=pf[fbank][:], in1=sg_sb[sl][:],
                op=ALU.mult))(fbank, sl, q),
                  reads=(f"pf{fbank}", f"sg{sl}"), writes=(f"gt{q}",))
    def quadrants_wo(g):
        par = g % 2
        xpar = g % 3
        c = g - NCH
        for q, (tb, oc) in enumerate([(0, 0), (0, 1), (1, 0), (1, 1)]):
            bank = 2 + oc
            for eb in range(CB):
                p.add("T", (lambda par, tb, oc, eb, bank: lambda e: e.matmul(
                    pf[bank][:],
                    outr[:, eb * TC + tb * P: eb * TC + (tb + 1) * P],
                    w_bf["wo"][:, eb * 1024 + oc * 512: eb * 1024 + (oc + 1) * 512],
                    start=(eb == 0), stop=(eb == CB - 1)))(par, tb, oc, eb, bank),
                      reads=("outr", f"wbf:wo:{eb}"),
                      writes=(f"pf{bank}",))
            p.add("V", (lambda bank, q: lambda e: e.tensor_tensor(
                out=gt_sb[q][:], in0=pf[bank][:], in1=gt_sb[q][:],
                op=ALU.mult))(bank, q),
                  reads=(f"pf{bank}", f"gt{q}"), writes=(f"gt{q}",))
            p.add("V", (lambda xpar, tb, oc, q: lambda e: e.tensor_tensor(
                out=x_buf[xpar][:, tb * 1024 + oc * 512: tb * 1024 + (oc + 1) * 512],
                in0=gt_sb[q][:],
                in1=x_buf[xpar][:, tb * 1024 + oc * 512: tb * 1024 + (oc + 1) * 512],
                op=ALU.add))(xpar, tb, oc, q),
                  reads=(f"gt{q}", f"x{xpar}"), writes=(f"x{xpar}",))
        p.add("Q", (lambda c, par, xpar: lambda e: e.dma_start(
            out=d_y[c * TC:(c + 1) * TC, :].rearrange("(tb p) d -> p tb d", p=P),
            in_=x_buf[xpar][:].rearrange("p (tb d) -> p tb d", tb=2)))(c, par, xpar),
              reads=(f"x{xpar}",), writes=(f"out{c}",), stream=f"qo{par}")

    def back_warm(g):
        transposes(g)
        kv_mms(g, True)
        stream_weight_dblk("wr", d_wr, g)
        stream_weight_dblk("wo", d_wo, g)
        emit_mix_jobs(8)
        scans(g, with_wkv=False)

    def head_main(g):
        transposes(g)
        kv_mms(g, False)
        r_mms(g)

    load_x(0)
    load_x(1)
    stats(0)
    norm1(0)
    stats(1)
    norm1(1)
    for g in range(NCH):
        stats(g + 2)
        norm1(g + 1) if g + 1 >= 2 else None
        back_warm(g)
    head_main(NCH)
    for g in range(NCH, NG):
        stats(g + 2)
        quadrants_ffn(g)
        norm1(g + 1)
        scans(g, with_wkv=True)
        if g + 1 < NG:
            head_main(g + 1)
        quadrants_wo(g)

    p.add("G", lambda e: e.engine_nop(),
          reads=tuple(f"out{c}" for c in range(NCH)), writes=())

    p.finalize()

    with nc.Block() as block:
        @block.tensor
        def _(eng):
            p.emit_engine("T", eng, sems)

        @block.vector
        def _(eng):
            p.emit_engine("V", eng, sems)

        @block.scalar
        def _(eng):
            p.emit_engine("S", eng, sems)

        @block.sync
        def _(eng):
            p.emit_engine("Q", eng, sems)

        @block.gpsimd
        def _(eng):
            p.emit_engine("G", eng, sems)

    es.close()
    return nc


_NC_CACHE = None


def kernel(L0_attn_x, L0_attn_Wk, L0_attn_Wv, L0_attn_Wr, L0_attn_Wo,
           L0_attn_time_decay, L0_attn_time_first,
           L0_ffn_W, L0_ffn_b, L0_ffn_ids, L0_ffn_weights):
    global _NC_CACHE
    x = np.asarray(L0_attn_x, np.float32)
    wkT = np.ascontiguousarray(np.asarray(L0_attn_Wk, np.float32).T)
    wvT = np.ascontiguousarray(np.asarray(L0_attn_Wv, np.float32).T)
    wrT = np.ascontiguousarray(np.asarray(L0_attn_Wr, np.float32).T)
    woT = np.ascontiguousarray(np.asarray(L0_attn_Wo, np.float32).T)
    ids = np.asarray(L0_ffn_ids).astype(np.int64)
    wselT = np.ascontiguousarray(
        np.asarray(L0_ffn_W, np.float32)[ids].transpose(0, 2, 1))
    bsel = np.ascontiguousarray(np.asarray(L0_ffn_b, np.float32)[ids])
    selw = np.ascontiguousarray(
        np.broadcast_to(np.asarray(L0_ffn_weights, np.float32), (P, 8)))
    selwk = np.ascontiguousarray(
        np.asarray(L0_ffn_weights, np.float32).reshape(8, 1))
    tdec = np.ascontiguousarray(
        np.asarray(L0_attn_time_decay, np.float32).reshape(8, P).T)
    tfir = np.ascontiguousarray(
        np.asarray(L0_attn_time_first, np.float32).reshape(8, P).T)
    identb = np.eye(P, dtype=np.float32).astype(ml_dtypes.bfloat16)
    ones1 = np.ones((1, P), dtype=ml_dtypes.bfloat16)

    if _NC_CACHE is None:
        _NC_CACHE = build_program()
    nc = _NC_CACHE

    in_maps = []
    for c in range(NCORE):
        b, h = c // 2, c % 2
        in_maps.append({
            "x": np.ascontiguousarray(x[b, h * TOK:(h + 1) * TOK, :]),
            "xw": np.ascontiguousarray(x[b, 0:TOK, :]),
            "wk": wkT, "wv": wvT, "wr": wrT, "wo": woT,
            "wsel": wselT, "bsel": bsel, "selw": selw, "selwk": selwk,
            "tdec": tdec, "tfir": tfir,
            "identb": identb, "ones1": ones1,
            "hmask": np.full((P, 1), float(h), np.float32),
        })

    res = run_bass_kernel_spmd(nc, in_maps, core_ids=list(range(NCORE)))
    out = np.empty((4, 4096, D), np.float32)
    for c in range(NCORE):
        b, h = c // 2, c % 2
        out[b, h * TOK:(h + 1) * TOK, :] = np.asarray(res.results[c]["y"])
    return out


# version 29 (final)
# speedup vs baseline: 1.0290x; 1.0290x over previous
import numpy as np
import ml_dtypes
from contextlib import ExitStack

import concourse.bass as bass
import concourse.mybir as mybir
from concourse.bass_utils import run_bass_kernel_spmd

F32 = mybir.dt.float32
BF16 = mybir.dt.bfloat16
U32 = mybir.dt.uint32
AF = mybir.ActivationFunctionType
ALU = mybir.AluOpType
AX = mybir.AxisListType

NCORE = 8
P = 128
D = 1024
TOK = 2048
TC = 256
NCH = TOK // TC
CB = 8
DBLK = 8
EPS = 1e-5

ENGS = ("T", "V", "S", "Q", "G")

_DETECTOR_SAFE = False


class _Op:
    __slots__ = ("eng", "emit", "reads", "writes", "deps", "inc", "idx",
                 "val", "waits", "custom_sem", "amount", "stream", "issue_val")

    def __init__(self, eng, emit, reads, writes, custom_sem=None):
        self.eng = eng
        self.stream = None
        self.emit = emit
        self.reads = tuple(reads)
        self.writes = tuple(writes)
        self.custom_sem = custom_sem
        self.deps = None
        self.inc = False
        self.val = None
        self.waits = None
        self.amount = 16 if eng in ("Q",) else 1


class Prog:
    def __init__(self):
        self.ops = []

    def add(self, eng, emit, reads=(), writes=(), custom_sem=None, amount=None,
            stream=None):
        op = _Op(eng, emit, reads, writes, custom_sem)
        if amount is not None:
            op.amount = amount
        if eng == "Q":
            op.stream = stream or "qs"
        op.idx = len(self.ops)
        self.ops.append(op)
        return op

    def finalize(self):
        last_writer = {}
        readers = {}
        for op in self.ops:
            deps = set()
            for r in op.reads:
                w = last_writer.get(r)
                if w is not None and w is not op:
                    deps.add(w)
            for w in op.writes:
                pw = last_writer.get(w)
                if pw is not None and pw is not op:
                    deps.add(pw)
                for rd in readers.get(w, ()):
                    if rd is not op:
                        deps.add(rd)
            for r in op.reads:
                readers.setdefault(r, []).append(op)
            for w in op.writes:
                last_writer[w] = op
                readers[w] = []
            op.deps = [d for d in deps
                       if not (d.eng == op.eng and op.eng == "T")
                       and not (op.eng == "Q" and d.eng == "Q"
                                and d.stream == op.stream)]
        for op in self.ops:
            for d in op.deps:
                if d.custom_sem is None:
                    d.inc = True
        counters = {e: 0 for e in ENGS}
        qcounters = {}
        for op in self.ops:
            if op.eng == "Q":
                op.inc = True
                qcounters[op.stream] = qcounters.get(op.stream, 0) + op.amount
                op.val = qcounters[op.stream]
                op.issue_val = op.val
            elif op.custom_sem is not None:
                op.val = None
            elif op.inc:
                counters[op.eng] += op.amount
                op.val = counters[op.eng]
            else:
                op.val = None
        qs_total = qcounters.get("qs", 0)
        for op in self.ops:
            if op.eng == "Q" and op.stream == "qs":
                op.val = qs_total
        marks = {e: {f: 0 for f in ENGS} for e in ENGS}
        cmarks = {e: {} for e in ENGS}
        for op in self.ops:
            waits = []
            need = {}
            cneed = {}
            for d in op.deps:
                if d.custom_sem is not None:
                    nm, v = d.custom_sem
                    cneed[nm] = max(cneed.get(nm, 0), v)
                elif d.eng == "Q":
                    cneed[d.stream] = max(cneed.get(d.stream, 0), d.val)
                else:
                    assert d.val is not None
                    need[d.eng] = max(need.get(d.eng, 0), d.val)
            for e, v in sorted(need.items()):
                if v > marks[op.eng][e]:
                    waits.append((e, v))
                    marks[op.eng][e] = v
            for nm, v in sorted(cneed.items()):
                if v > cmarks[op.eng].get(nm, 0):
                    waits.append((nm, v))
                    cmarks[op.eng][nm] = v
            op.waits = waits

    def emit_engine(self, eng_name, eng, sems):
        for op in self.ops:
            if op.eng != eng_name:
                continue
            for tgt, v in op.waits:
                eng.wait_ge(sems[tgt], v)
            if _DETECTOR_SAFE and op.eng == "Q" and op.issue_val > op.amount:
                eng.wait_ge(sems[op.stream], op.issue_val - op.amount)
            inst = op.emit(eng)
            if op.eng == "Q":
                inst.then_inc(sems[op.stream], op.amount)
            elif op.inc and op.custom_sem is None:
                assert inst is not None, f"op {op.idx} on {eng_name} needs inc"
                inst.then_inc(sems[op.eng], op.amount)


def build_program():
    nc = bass.Bass("TRN2", target_bir_lowering=False,
                   detect_race_conditions=False)

    d_x = nc.dram_tensor("x", [TOK, D], F32, kind="ExternalInput").ap()
    d_wk = nc.dram_tensor("wk", [D, D], F32, kind="ExternalInput").ap()
    d_wv = nc.dram_tensor("wv", [D, D], F32, kind="ExternalInput").ap()
    d_wr = nc.dram_tensor("wr", [D, D], F32, kind="ExternalInput").ap()
    d_wo = nc.dram_tensor("wo", [D, D], F32, kind="ExternalInput").ap()
    d_wsel = nc.dram_tensor("wsel", [8, D, D], F32, kind="ExternalInput").ap()
    d_bsel = nc.dram_tensor("bsel", [8, D], F32, kind="ExternalInput").ap()
    d_selw = nc.dram_tensor("selw", [P, 8], F32, kind="ExternalInput").ap()
    d_selwk = nc.dram_tensor("selwk", [8, 1], F32, kind="ExternalInput").ap()
    d_tdec = nc.dram_tensor("tdec", [P, 8], F32, kind="ExternalInput").ap()
    d_tfir = nc.dram_tensor("tfir", [P, 8], F32, kind="ExternalInput").ap()
    d_identb = nc.dram_tensor("identb", [P, P], BF16, kind="ExternalInput").ap()
    d_ones1 = nc.dram_tensor("ones1", [1, P], BF16, kind="ExternalInput").ap()
    d_hmask = nc.dram_tensor("hmask", [P, 1], F32, kind="ExternalInput").ap()
    d_xw = nc.dram_tensor("xw", [TOK, D], F32, kind="ExternalInput").ap()
    d_y = nc.dram_tensor("y", [TOK, D], F32, kind="ExternalOutput").ap()

    es = ExitStack()
    sb = lambda name, shape, dt: es.enter_context(nc.sbuf_tensor(name, shape, dt))

    x_buf = [sb(f"x_buf{i}", [P, 2048], F32) for i in range(3)]
    xn_buf = [sb(f"xn_buf{i}", [P, 2048], BF16) for i in range(2)]
    xnt_buf = [sb(f"xnt_buf{i}", [P, 2048], BF16) for i in range(2)]
    w_bf = {n: sb(f"w_{n}", [P, 8192], BF16) for n in
            ("wk", "wv", "wr", "wo", "wm")}
    wstage = [sb(f"wstage{i}", [P, 1024], F32) for i in range(4)]
    wselbf = [sb(f"wselbf{i}", [P, 1024], BF16) for i in range(2)]
    kv_buf = sb("kv_buf", [P, 2048], F32)
    y_buf = sb("y_buf", [P, 8 * 260], F32)
    r_sb = sb("r_sb", [P, 2048], BF16)
    outr = sb("outr", [P, 2048], BF16)
    wkvf = [sb(f"wkvf{i}", [P, TC], F32) for i in range(2)]
    sg_sb = [sb(f"sg_sb{i}", [P, 512], BF16) for i in range(2)]
    gt_sb = [sb(f"gt_sb{i}", [P, 512], F32) for i in range(4)]
    k_sb = [sb(f"k_sb{i}", [P, TC], F32) for i in range(2)]
    sqscr = sb("sqscr", [P, 1024], BF16)
    s_tdec = sb("s_tdec", [P, 8], F32)
    s_tfir = sb("s_tfir", [P, 8], F32)
    s_expd = sb("s_expd", [P, 8], F32)
    s_wneg = sb("s_wneg", [P, 8], F32)
    s_dec8 = sb("s_dec8", [P, 8], F32)
    s_eu8 = sb("s_eu8", [P, 8], F32)
    s_ident = sb("s_ident", [P, P], BF16)
    s_ones1 = sb("s_ones1", [1, P], BF16)
    s_selw = sb("s_selw", [P, 8], F32)
    s_selwk = sb("s_selwk", [8, 1], F32)
    s_selwkb = sb("s_selwkb", [8, 1], BF16)
    s_bselb = sb("s_bselb", [8, 1024], BF16)
    s_biasrow = sb("s_biasrow", [1, 1024], BF16)
    s_wik = sb("s_wik", [P, 8 * P], BF16)
    s_hmask = sb("s_hmask", [P, 1], F32)
    s_sum = sb("s_sum", [P, 2], F32)
    s_mean = sb("s_mean", [P, 4], F32)
    s_sumsq = sb("s_sumsq", [P, 2], F32)
    s_var = sb("s_var", [P, 4], F32)
    s_msq = sb("s_msq", [P, 2], F32)
    s_lnv = sb("s_lnv", [P, 4], F32)
    s_rstd = sb("s_rstd", [P, 4], F32)
    s_nmr = sb("s_nmr", [P, 4], F32)

    pf = [es.enter_context(nc.psum_tensor(f"pf{i}", [P, 512], F32)) for i in range(6)]
    pb = [es.enter_context(nc.psum_tensor(f"pbt{i}", [P, 1024], BF16)) for i in range(2)]

    sems = {}
    for name in (list(ENGS) + ["qs", "qx0", "qx1", "qx2", "qo0", "qo1"]
                 + [f"qw{i}" for i in range(4)]):
        sems[name] = es.enter_context(nc.semaphore(f"sem_{name}"))

    p = Prog()

    smalls = [
        (s_tdec, d_tdec), (s_tfir, d_tfir),
        (s_ident, d_identb), (s_ones1, d_ones1), (s_selw, d_selw),
        (s_selwk, d_selwk), (s_hmask, d_hmask),
    ]
    for dst, src in smalls:
        p.add("Q", (lambda dst, src: lambda e: e.dma_start(out=dst[:], in_=src))(dst, src),
              reads=(), writes=(f"sb:{dst.name}",), stream="qs")

    p.add("Q", lambda e: e.dma_start(out=wstage[0][0:8, :], in_=d_bsel),
          reads=("wstage0",), writes=("wstage0", "bselw"), stream="qs")
    p.add("V", lambda e: e.tensor_copy(s_bselb[:], wstage[0][0:8, :]),
          reads=("bselw",), writes=("sb:s_bselb", "wstage0"))

    p.add("S", lambda e: e.activation(s_expd[:], s_tdec[:], AF.Exp),
          reads=("sb:s_tdec",), writes=("sb:s_expd",))
    p.add("V", lambda e: e.tensor_scalar(out=s_wneg[:], in0=s_expd[:], scalar1=-1.0,
                                         scalar2=None, op0=ALU.mult),
          reads=("sb:s_expd",), writes=("sb:s_wneg",))
    p.add("S", lambda e: e.activation(s_dec8[:], s_wneg[:], AF.Exp),
          reads=("sb:s_wneg",), writes=("sb:s_dec8",))
    p.add("S", lambda e: e.activation(s_eu8[:], s_tfir[:], AF.Exp),
          reads=("sb:s_tfir",), writes=("sb:s_eu8",))
    p.add("V", lambda e: e.memset(y_buf[:], 0.0), reads=(), writes=("y_init",))

    p.add("V", lambda e: e.tensor_copy(s_selwkb[:], s_selwk[:]),
          reads=("sb:s_selwk",), writes=("sb:s_selwkb",))
    for oc in range(2):
        p.add("T", (lambda oc: lambda e: e.matmul(
            pf[0][0:1, :], s_selwkb[:], s_bselb[:, oc * 512:(oc + 1) * 512],
            start=True, stop=True))(oc),
              reads=("sb:s_selwkb", "sb:s_bselb"), writes=("pf0",))
        p.add("S", (lambda oc: lambda e: e.copy(
            s_biasrow[0:1, oc * 512:(oc + 1) * 512], pf[0][0:1, :]))(oc),
              reads=("pf0",), writes=("sb:s_biasrow",))
    for k in range(8):
        p.add("V", (lambda k: lambda e: e.tensor_scalar(
            out=s_wik[:, k * P:(k + 1) * P], in0=s_ident[:],
            scalar1=s_selw[:, k:k + 1], scalar2=None, op0=ALU.mult))(k),
              reads=("sb:s_ident", "sb:s_selw"), writes=(f"wik{k}",))

    def w_load(dram, dblk, slot):
        return lambda e: e.dma_start(out=wstage[slot][:],
                                     in_=dram[dblk * P:(dblk + 1) * P, :])

    def w_cast(name, dblk, slot):
        return lambda e: e.tensor_copy(
            w_bf[name][:, dblk * 1024:(dblk + 1) * 1024], wstage[slot][:])

    slot_ctr = [0]

    def stream_weight(name, dram):
        for dblk in range(DBLK):
            sl = slot_ctr[0] % 4
            slot_ctr[0] += 1
            p.add("Q", w_load(dram, dblk, sl), reads=(f"wstage{sl}",),
                  writes=(f"wstage{sl}w",), stream=f"qw{sl}")
            p.add("V", w_cast(name, dblk, sl), reads=(f"wstage{sl}w",),
                  writes=(f"wstage{sl}", f"wbf:{name}:{dblk}"))

    def x_load(c, tag):
        buf = x_buf[c % 2]
        return lambda e: e.dma_start(
            out=buf[:].rearrange("p (tb d) -> p tb d", tb=2),
            in_=d_x[c * TC:(c + 1) * TC, :].rearrange("(tb p) d -> p tb d", p=P))

    stream_weight("wk", d_wk)
    stream_weight("wv", d_wv)

    def stream_weight_dblk(name, dram, dblk):
        sl = slot_ctr[0] % 4
        slot_ctr[0] += 1
        p.add("Q", w_load(dram, dblk, sl), reads=(f"wstage{sl}",),
              writes=(f"wstage{sl}w",), stream=f"qw{sl}")
        p.add("V", w_cast(name, dblk, sl), reads=(f"wstage{sl}w",),
              writes=(f"wstage{sl}", f"wbf:{name}:{dblk}"))

    mix_jobs = []
    for dblk in range(DBLK):
        for k in range(8):
            mix_jobs.append((dblk, k))
    mix_ptr = [0]

    def emit_mix_jobs(n):
        for _ in range(n):
            if mix_ptr[0] >= len(mix_jobs):
                return
            dblk, k = mix_jobs[mix_ptr[0]]
            mix_ptr[0] += 1
            sl = slot_ctr[0] % 4
            slot_ctr[0] += 1
            p.add("Q", (lambda dblk, k, sl: lambda e: e.dma_start(
                out=wstage[sl][:],
                in_=d_wsel[k, dblk * P:(dblk + 1) * P, :]))(dblk, k, sl),
                  reads=(f"wstage{sl}",), writes=(f"wstage{sl}w",), stream=f"qw{sl}")
            wsl = mix_ptr[0] % 2
            if mix_ptr[0] % 2 == 0:
                p.add("V", (lambda sl, wsl: lambda e: e.tensor_copy(
                    wselbf[wsl][:], wstage[sl][:]))(sl, wsl),
                      reads=(f"wstage{sl}w",), writes=(f"wstage{sl}", f"wselbf{wsl}"))
            else:
                p.add("S", (lambda sl, wsl: lambda e: e.copy(
                    wselbf[wsl][:], wstage[sl][:]))(sl, wsl),
                      reads=(f"wstage{sl}w",), writes=(f"wstage{sl}", f"wselbf{wsl}"))
            for oc in range(2):
                bank = 4 + oc
                p.add("T", (lambda dblk, k, oc, bank, wsl: lambda e: e.matmul(
                    pf[bank][:], s_wik[:, k * P:(k + 1) * P],
                    wselbf[wsl][:, oc * 512:(oc + 1) * 512],
                    start=(k == 0), stop=(k == 7)))(dblk, k, oc, bank, wsl),
                      reads=(f"wik{k}", f"wselbf{wsl}"), writes=(f"pf{bank}",))
            if k == 7:
                for oc in range(2):
                    bank = 4 + oc
                    p.add("S", (lambda dblk, oc, bank: lambda e: e.copy(
                        w_bf["wm"][:, dblk * 1024 + oc * 512: dblk * 1024 + (oc + 1) * 512],
                        pf[bank][:]))(dblk, oc, bank),
                          reads=(f"pf{bank}",), writes=(f"wbf:wm:{dblk}",))

    def layernorm(c, phase):
        buf = x_buf[c % 3]
        xr = f"x{c % 3}"
        for tb in range(2):
            tile = buf[:, tb * 1024:(tb + 1) * 1024]
            p.add("V", (lambda tile, tb: lambda e: e.tensor_reduce(
                out=s_sum[:, tb:tb + 1], in_=tile, axis=AX.X, op=ALU.add))(tile, tb),
                  reads=(xr,), writes=("lnsum",))
            p.add("S", (lambda c, tile, tb: lambda e: e.activation(
                sqscr[:], tile, AF.Square,
                accum_out=s_sumsq[:, tb:tb + 1]))(c, tile, tb),
                  reads=(xr,), writes=("sqscr", "lnsumsq"))
        pr = c % 2
        p.add("V", (lambda pr: lambda e: e.tensor_scalar(
            out=s_mean[:, 2 * pr:2 * pr + 2], in0=s_sum[:],
            scalar1=1.0 / D, scalar2=None, op0=ALU.mult))(pr),
              reads=("lnsum",), writes=(f"lnmean{pr}",))
        p.add("V", (lambda pr: lambda e: e.tensor_scalar(
            out=s_var[:, 2 * pr:2 * pr + 2], in0=s_sumsq[:],
            scalar1=1.0 / D, scalar2=EPS,
            op0=ALU.mult, op1=ALU.add))(pr),
              reads=("lnsumsq",), writes=(f"lnvar{pr}",))
        p.add("V", (lambda pr: lambda e: e.tensor_tensor(
            out=s_msq[:], in0=s_mean[:, 2 * pr:2 * pr + 2],
            in1=s_mean[:, 2 * pr:2 * pr + 2], op=ALU.mult))(pr),
              reads=(f"lnmean{pr}",), writes=("lnmsq",))
        p.add("V", (lambda pr: lambda e: e.tensor_tensor(
            out=s_var[:, 2 * pr:2 * pr + 2], in0=s_var[:, 2 * pr:2 * pr + 2],
            in1=s_msq[:], op=ALU.subtract))(pr),
              reads=(f"lnvar{pr}", "lnmsq"), writes=(f"lnvar{pr}",))

    def rstd_fin_pair():
        p.add("S", lambda e: e.activation(s_lnv[:], s_var[:], AF.Ln),
              reads=("lnvar0", "lnvar1"), writes=("lnlnv",))
        p.add("S", lambda e: e.activation(
            s_rstd[:], s_lnv[:], AF.Exp, scale=-0.5),
              reads=("lnlnv",), writes=("rstdc0", "rstdc1"))
        p.add("V", lambda e: e.scalar_tensor_tensor(
            out=s_nmr[:], in0=s_mean[:], scalar=-1.0,
            in1=s_rstd[:], op0=ALU.mult, op1=ALU.mult),
              reads=("lnmean0", "lnmean1", "rstdc0", "rstdc1"),
              writes=("nmrc0", "nmrc1"))

    def normalize(c):
        buf = x_buf[c % 3]
        xr = f"x{c % 3}"
        for tb in range(2):
            pr = c % 2
            p.add("S", (lambda c, tb, buf, pr: lambda e: e.activation(
                xn_buf[c % 2][:, tb * 1024:(tb + 1) * 1024],
                buf[:, tb * 1024:(tb + 1) * 1024], AF.Identity,
                bias=s_nmr[:, 2 * pr + tb:2 * pr + tb + 1],
                scale=s_rstd[:, 2 * pr + tb:2 * pr + tb + 1]))(c, tb, buf, pr),
                  reads=(xr, f"rstdc{pr}", f"nmrc{pr}"),
                  writes=(f"xn{c % 2}t{tb}",))

    def transposes(c):
        par = c % 2
        for dblk in range(DBLK):
            bank = dblk % 2
            for tb in range(2):
                p.add("T", (lambda par, dblk, tb, bank: lambda e: e.transpose(
                    pb[bank][:, tb * P:(tb + 1) * P],
                    xn_buf[par][:, tb * 1024 + dblk * P: tb * 1024 + (dblk + 1) * P],
                    s_ident[:]))(par, dblk, tb, bank),
                      reads=(f"xn{par}t{tb}", "sb:s_ident"),
                      writes=(f"pbt{bank}",))
            p.add("S", (lambda par, dblk, bank: lambda e: e.copy(
                xnt_buf[par][:, dblk * TC:(dblk + 1) * TC],
                pb[bank][:, 0:TC]))(par, dblk, bank),
                  reads=(f"pbt{bank}",), writes=(f"xnt{par}d{dblk}",))

    NG = 2 * NCH

    def load_x(g):
        par = g % 3
        dram = d_xw if g < NCH else d_x
        c = g % NCH
        p.add("Q", (lambda dram, c, par: lambda e: e.dma_start(
            out=x_buf[par][:].rearrange("p (tb d) -> p tb d", tb=2),
            in_=dram[c * TC:(c + 1) * TC, :].rearrange("(tb p) d -> p tb d", p=P)))(dram, c, par),
              reads=(f"x{par}",), writes=(f"x{par}",), stream=f"qx{par}")

    def stats(g):
        if g >= NG:
            return
        if g > 1:
            load_x(g)
        layernorm(g, "m")

    def norm1(g):
        if g >= NG:
            return
        normalize(g)

    def kv_mms(g, warm):
        par = g % 2
        for eb in range(CB):
            kbank = 0 + (eb % 2)
            vbank = 2 + (eb % 2)
            for db in range(DBLK):
                p.add("T", (lambda par, eb, db, kbank: lambda e: e.matmul(
                    pf[kbank][:, 0:TC],
                    w_bf["wk"][:, db * 1024 + eb * P: db * 1024 + (eb + 1) * P],
                    xnt_buf[par][:, db * TC:(db + 1) * TC],
                    start=(db == 0), stop=(db == DBLK - 1)))(par, eb, db, kbank),
                      reads=(f"wbf:wk:{db}", f"xnt{par}d{db}"),
                      writes=(f"pf{kbank}",))
            for db in range(DBLK):
                p.add("T", (lambda par, eb, db, vbank: lambda e: e.matmul(
                    pf[vbank][:, 0:TC],
                    w_bf["wv"][:, db * 1024 + eb * P: db * 1024 + (eb + 1) * P],
                    xnt_buf[par][:, db * TC:(db + 1) * TC],
                    start=(db == 0), stop=(db == DBLK - 1)))(par, eb, db, vbank),
                      reads=(f"wbf:wv:{db}", f"xnt{par}d{db}"),
                      writes=(f"pf{vbank}",))
            ksl = eb % 2
            if warm:
                p.add("S", (lambda eb, kbank, ksl: lambda e: e.activation(
                    k_sb[ksl][:], pf[kbank][:, 0:TC], AF.Identity,
                    scale=s_hmask[:, 0:1]))(eb, kbank, ksl),
                      reads=(f"pf{kbank}", "sb:s_hmask"), writes=(f"ksb{ksl}",))
            else:
                p.add("S", (lambda eb, kbank, ksl: lambda e: e.copy(
                    k_sb[ksl][:], pf[kbank][:, 0:TC]))(eb, kbank, ksl),
                      reads=(f"pf{kbank}",), writes=(f"ksb{ksl}",))
            p.add("V", (lambda eb, vbank, ksl: lambda e: e.tensor_tensor(
                out=kv_buf[:, eb * TC:(eb + 1) * TC], in0=k_sb[ksl][:],
                in1=pf[vbank][:, 0:TC], op=ALU.mult))(eb, vbank, ksl),
                  reads=(f"ksb{ksl}", f"pf{vbank}"), writes=(f"kv{eb}",))

    def scans(g, with_wkv):
        for cb in range(CB):
            ybase = cb * 260
            if g > 0:
                p.add("V", (lambda cb, ybase: lambda e: e.tensor_copy(
                    y_buf[:, ybase:ybase + 1], y_buf[:, ybase + TC:ybase + TC + 1]))(cb, ybase),
                      reads=(f"y{cb}",), writes=(f"y{cb}c",))
            init = 0.0 if g == 0 else None
            p.add("V", (lambda cb, ybase, init: lambda e: e.tensor_tensor_scan(
                out=y_buf[:, ybase + 1:ybase + 1 + TC],
                data0=s_dec8[:, cb:cb + 1].broadcast_to((P, TC)),
                data1=kv_buf[:, cb * TC:(cb + 1) * TC],
                initial=(y_buf[:, ybase:ybase + 1] if init is None else 0.0),
                op0=ALU.mult, op1=ALU.add))(cb, ybase, init),
                  reads=("sb:s_dec8", f"kv{cb}", f"y{cb}c", "y_init"),
                  writes=(f"y{cb}",))
            if with_wkv:
                sl = cb % 2
                p.add("V", (lambda cb, sl: lambda e: e.tensor_scalar(
                    out=wkvf[sl][:], in0=kv_buf[:, cb * TC:(cb + 1) * TC],
                    scalar1=s_eu8[:, cb:cb + 1], scalar2=None, op0=ALU.mult))(cb, sl),
                      reads=(f"kv{cb}", "sb:s_eu8"), writes=(f"wkvf{sl}",))
                p.add("V", (lambda cb, ybase, sl: lambda e: e.tensor_tensor(
                    out=wkvf[sl][:], in0=wkvf[sl][:],
                    in1=y_buf[:, ybase:ybase + TC], op=ALU.add))(cb, ybase, sl),
                      reads=(f"wkvf{sl}", f"y{cb}", f"y{cb}c", "y_init"),
                      writes=(f"wkvf{sl}",))
                p.add("V", (lambda cb, sl: lambda e: e.tensor_tensor(
                    out=outr[:, cb * TC:(cb + 1) * TC], in0=wkvf[sl][:],
                    in1=r_sb[:, cb * TC:(cb + 1) * TC], op=ALU.mult))(cb, sl),
                      reads=(f"wkvf{sl}", "rsball"), writes=("outr",))

    def r_mms(g):
        par = g % 2
        for eb in range(CB):
            rbank = 0 + (eb % 2)
            for db in range(DBLK):
                p.add("T", (lambda par, eb, db, rbank: lambda e: e.matmul(
                    pf[rbank][:, 0:TC],
                    w_bf["wr"][:, db * 1024 + eb * P: db * 1024 + (eb + 1) * P],
                    xnt_buf[par][:, db * TC:(db + 1) * TC],
                    start=(db == 0), stop=(db == DBLK - 1)))(par, eb, db, rbank),
                      reads=(f"wbf:wr:{db}", f"xnt{par}d{db}"),
                      writes=(f"pf{rbank}",))
            p.add("S", (lambda eb, rbank: lambda e: e.activation(
                r_sb[:, eb * TC:(eb + 1) * TC], pf[rbank][:, 0:TC],
                AF.Sigmoid))(eb, rbank),
                  reads=(f"pf{rbank}",), writes=("rsball",))

    def quadrants_ffn(g):
        par = g % 2
        for q, (tb, oc) in enumerate([(0, 0), (0, 1), (1, 0), (1, 1)]):
            fbank = 4 + oc
            for db in range(DBLK):
                p.add("T", (lambda par, tb, oc, db, fbank: lambda e: e.matmul(
                    pf[fbank][:],
                    xnt_buf[par][:, db * TC + tb * P: db * TC + (tb + 1) * P],
                    w_bf["wm"][:, db * 1024 + oc * 512: db * 1024 + (oc + 1) * 512],
                    start=(db == 0), stop=False))(par, tb, oc, db, fbank),
                      reads=(f"xnt{par}d{db}", f"wbf:wm:{db}"),
                      writes=(f"pf{fbank}",))
            p.add("T", (lambda tb, oc, fbank: lambda e: e.matmul(
                pf[fbank][:], s_ones1[:],
                s_biasrow[0:1, oc * 512:(oc + 1) * 512],
                start=False, stop=True))(tb, oc, fbank),
                  reads=("sb:s_ones1", "sb:s_biasrow"),
                  writes=(f"pf{fbank}",))
            sl = oc
            p.add("S", (lambda fbank, sl: lambda e: e.activation(
                sg_sb[sl][:], pf[fbank][:], AF.Sigmoid))(fbank, sl),
                  reads=(f"pf{fbank}",), writes=(f"sg{sl}",))
            p.add("V", (lambda fbank, sl, q: lambda e: e.tensor_tensor(
                out=gt_sb[q][:], in0=pf[fbank][:], in1=sg_sb[sl][:],
                op=ALU.mult))(fbank, sl, q),
                  reads=(f"pf{fbank}", f"sg{sl}"), writes=(f"gt{q}",))
    def quadrants_wo(g):
        par = g % 2
        xpar = g % 3
        c = g - NCH
        for q, (tb, oc) in enumerate([(0, 0), (0, 1), (1, 0), (1, 1)]):
            bank = 2 + oc
            for eb in range(CB):
                p.add("T", (lambda par, tb, oc, eb, bank: lambda e: e.matmul(
                    pf[bank][:],
                    outr[:, eb * TC + tb * P: eb * TC + (tb + 1) * P],
                    w_bf["wo"][:, eb * 1024 + oc * 512: eb * 1024 + (oc + 1) * 512],
                    start=(eb == 0), stop=(eb == CB - 1)))(par, tb, oc, eb, bank),
                      reads=("outr", f"wbf:wo:{eb}"),
                      writes=(f"pf{bank}",))
            p.add("V", (lambda bank, q: lambda e: e.tensor_tensor(
                out=gt_sb[q][:], in0=pf[bank][:], in1=gt_sb[q][:],
                op=ALU.mult))(bank, q),
                  reads=(f"pf{bank}", f"gt{q}"), writes=(f"gt{q}",))
            p.add("V", (lambda xpar, tb, oc, q: lambda e: e.tensor_tensor(
                out=x_buf[xpar][:, tb * 1024 + oc * 512: tb * 1024 + (oc + 1) * 512],
                in0=gt_sb[q][:],
                in1=x_buf[xpar][:, tb * 1024 + oc * 512: tb * 1024 + (oc + 1) * 512],
                op=ALU.add))(xpar, tb, oc, q),
                  reads=(f"gt{q}", f"x{xpar}"), writes=(f"x{xpar}",))
        p.add("Q", (lambda c, par, xpar: lambda e: e.dma_start(
            out=d_y[c * TC:(c + 1) * TC, :].rearrange("(tb p) d -> p tb d", p=P),
            in_=x_buf[xpar][:].rearrange("p (tb d) -> p tb d", tb=2)))(c, par, xpar),
              reads=(f"x{xpar}",), writes=(f"out{c}",), stream=f"qo{par}")

    def back_warm(g):
        transposes(g)
        kv_mms(g, True)
        stream_weight_dblk("wr", d_wr, g)
        stream_weight_dblk("wo", d_wo, g)
        emit_mix_jobs(8)
        scans(g, with_wkv=False)

    def head_main(g):
        transposes(g)
        kv_mms(g, False)
        r_mms(g)

    load_x(0)
    load_x(1)
    stats(0)
    stats(1)
    rstd_fin_pair()
    norm1(0)
    norm1(1)
    for g in range(NCH):
        stats(g + 2)
        if (g + 2) % 2 == 1:
            rstd_fin_pair()
        norm1(g + 1) if g + 1 >= 2 else None
        back_warm(g)
    head_main(NCH)
    for g in range(NCH, NG):
        stats(g + 2)
        if g + 2 < NG and (g + 2) % 2 == 1:
            rstd_fin_pair()
        quadrants_ffn(g)
        norm1(g + 1)
        scans(g, with_wkv=True)
        if g + 1 < NG:
            head_main(g + 1)
        quadrants_wo(g)

    p.add("G", lambda e: e.engine_nop(),
          reads=tuple(f"out{c}" for c in range(NCH)), writes=())

    p.finalize()

    with nc.Block() as block:
        @block.tensor
        def _(eng):
            p.emit_engine("T", eng, sems)

        @block.vector
        def _(eng):
            p.emit_engine("V", eng, sems)

        @block.scalar
        def _(eng):
            p.emit_engine("S", eng, sems)

        @block.sync
        def _(eng):
            p.emit_engine("Q", eng, sems)

        @block.gpsimd
        def _(eng):
            p.emit_engine("G", eng, sems)

    es.close()
    return nc


_NC_CACHE = None


def kernel(L0_attn_x, L0_attn_Wk, L0_attn_Wv, L0_attn_Wr, L0_attn_Wo,
           L0_attn_time_decay, L0_attn_time_first,
           L0_ffn_W, L0_ffn_b, L0_ffn_ids, L0_ffn_weights):
    global _NC_CACHE
    x = np.asarray(L0_attn_x, np.float32)
    wkT = np.ascontiguousarray(np.asarray(L0_attn_Wk, np.float32).T)
    wvT = np.ascontiguousarray(np.asarray(L0_attn_Wv, np.float32).T)
    wrT = np.ascontiguousarray(np.asarray(L0_attn_Wr, np.float32).T)
    woT = np.ascontiguousarray(np.asarray(L0_attn_Wo, np.float32).T)
    ids = np.asarray(L0_ffn_ids).astype(np.int64)
    wselT = np.ascontiguousarray(
        np.asarray(L0_ffn_W, np.float32)[ids].transpose(0, 2, 1))
    bsel = np.ascontiguousarray(np.asarray(L0_ffn_b, np.float32)[ids])
    selw = np.ascontiguousarray(
        np.broadcast_to(np.asarray(L0_ffn_weights, np.float32), (P, 8)))
    selwk = np.ascontiguousarray(
        np.asarray(L0_ffn_weights, np.float32).reshape(8, 1))
    tdec = np.ascontiguousarray(
        np.asarray(L0_attn_time_decay, np.float32).reshape(8, P).T)
    tfir = np.ascontiguousarray(
        np.asarray(L0_attn_time_first, np.float32).reshape(8, P).T)
    identb = np.eye(P, dtype=np.float32).astype(ml_dtypes.bfloat16)
    ones1 = np.ones((1, P), dtype=ml_dtypes.bfloat16)

    if _NC_CACHE is None:
        _NC_CACHE = build_program()
    nc = _NC_CACHE

    in_maps = []
    for c in range(NCORE):
        b, h = c // 2, c % 2
        in_maps.append({
            "x": np.ascontiguousarray(x[b, h * TOK:(h + 1) * TOK, :]),
            "xw": np.ascontiguousarray(x[b, 0:TOK, :]),
            "wk": wkT, "wv": wvT, "wr": wrT, "wo": woT,
            "wsel": wselT, "bsel": bsel, "selw": selw, "selwk": selwk,
            "tdec": tdec, "tfir": tfir,
            "identb": identb, "ones1": ones1,
            "hmask": np.full((P, 1), float(h), np.float32),
        })

    res = run_bass_kernel_spmd(nc, in_maps, core_ids=list(range(NCORE)))
    out = np.empty((4, 4096, D), np.float32)
    for c in range(NCORE):
        b, h = c // 2, c % 2
        out[b, h * TOK:(h + 1) * TOK, :] = np.asarray(res.results[c]["y"])
    return out


# version 30
# speedup vs baseline: 1.0378x; 1.0085x over previous
import numpy as np
import ml_dtypes
from contextlib import ExitStack

import concourse.bass as bass
import concourse.mybir as mybir
from concourse.bass_utils import run_bass_kernel_spmd

F32 = mybir.dt.float32
BF16 = mybir.dt.bfloat16
U32 = mybir.dt.uint32
AF = mybir.ActivationFunctionType
ALU = mybir.AluOpType
AX = mybir.AxisListType

NCORE = 8
P = 128
D = 1024
TOK = 2048
TC = 256
NCH = TOK // TC
CB = 8
DBLK = 8
EPS = 1e-5

ENGS = ("T", "V", "S", "Q", "G")

_DETECTOR_SAFE = False


class _Op:
    __slots__ = ("eng", "emit", "reads", "writes", "deps", "inc", "idx",
                 "val", "waits", "custom_sem", "amount", "stream", "issue_val")

    def __init__(self, eng, emit, reads, writes, custom_sem=None):
        self.eng = eng
        self.stream = None
        self.emit = emit
        self.reads = tuple(reads)
        self.writes = tuple(writes)
        self.custom_sem = custom_sem
        self.deps = None
        self.inc = False
        self.val = None
        self.waits = None
        self.amount = 16 if eng in ("Q",) else 1


class Prog:
    def __init__(self):
        self.ops = []

    def add(self, eng, emit, reads=(), writes=(), custom_sem=None, amount=None,
            stream=None):
        op = _Op(eng, emit, reads, writes, custom_sem)
        if amount is not None:
            op.amount = amount
        if eng == "Q":
            op.stream = stream or "qs"
        op.idx = len(self.ops)
        self.ops.append(op)
        return op

    def finalize(self):
        last_writer = {}
        readers = {}
        for op in self.ops:
            deps = set()
            for r in op.reads:
                w = last_writer.get(r)
                if w is not None and w is not op:
                    deps.add(w)
            for w in op.writes:
                pw = last_writer.get(w)
                if pw is not None and pw is not op:
                    deps.add(pw)
                for rd in readers.get(w, ()):
                    if rd is not op:
                        deps.add(rd)
            for r in op.reads:
                readers.setdefault(r, []).append(op)
            for w in op.writes:
                last_writer[w] = op
                readers[w] = []
            op.deps = [d for d in deps
                       if not (d.eng == op.eng and op.eng == "T")
                       and not (op.eng == "Q" and d.eng == "Q"
                                and d.stream == op.stream)]
        for op in self.ops:
            for d in op.deps:
                if d.custom_sem is None:
                    d.inc = True
        counters = {e: 0 for e in ENGS}
        qcounters = {}
        for op in self.ops:
            if op.eng == "Q":
                op.inc = True
                qcounters[op.stream] = qcounters.get(op.stream, 0) + op.amount
                op.val = qcounters[op.stream]
                op.issue_val = op.val
            elif op.custom_sem is not None:
                op.val = None
            elif op.inc:
                counters[op.eng] += op.amount
                op.val = counters[op.eng]
            else:
                op.val = None
        qs_total = qcounters.get("qs", 0)
        for op in self.ops:
            if op.eng == "Q" and op.stream == "qs":
                op.val = qs_total
        marks = {e: {f: 0 for f in ENGS} for e in ENGS}
        cmarks = {e: {} for e in ENGS}
        for op in self.ops:
            waits = []
            need = {}
            cneed = {}
            for d in op.deps:
                if d.custom_sem is not None:
                    nm, v = d.custom_sem
                    cneed[nm] = max(cneed.get(nm, 0), v)
                elif d.eng == "Q":
                    cneed[d.stream] = max(cneed.get(d.stream, 0), d.val)
                else:
                    assert d.val is not None
                    need[d.eng] = max(need.get(d.eng, 0), d.val)
            for e, v in sorted(need.items()):
                if v > marks[op.eng][e]:
                    waits.append((e, v))
                    marks[op.eng][e] = v
            for nm, v in sorted(cneed.items()):
                if v > cmarks[op.eng].get(nm, 0):
                    waits.append((nm, v))
                    cmarks[op.eng][nm] = v
            op.waits = waits

    def emit_engine(self, eng_name, eng, sems):
        for op in self.ops:
            if op.eng != eng_name:
                continue
            for tgt, v in op.waits:
                eng.wait_ge(sems[tgt], v)
            if _DETECTOR_SAFE and op.eng == "Q" and op.issue_val > op.amount:
                eng.wait_ge(sems[op.stream], op.issue_val - op.amount)
            inst = op.emit(eng)
            if op.eng == "Q":
                inst.then_inc(sems[op.stream], op.amount)
            elif op.inc and op.custom_sem is None:
                assert inst is not None, f"op {op.idx} on {eng_name} needs inc"
                inst.then_inc(sems[op.eng], op.amount)


def build_program():
    nc = bass.Bass("TRN2", target_bir_lowering=False,
                   detect_race_conditions=False)

    d_x = nc.dram_tensor("x", [TOK, D], F32, kind="ExternalInput").ap()
    d_wk = nc.dram_tensor("wk", [D, D], F32, kind="ExternalInput").ap()
    d_wv = nc.dram_tensor("wv", [D, D], F32, kind="ExternalInput").ap()
    d_wr = nc.dram_tensor("wr", [D, D], F32, kind="ExternalInput").ap()
    d_wo = nc.dram_tensor("wo", [D, D], F32, kind="ExternalInput").ap()
    d_wsel = nc.dram_tensor("wsel", [8, D, D], F32, kind="ExternalInput").ap()
    d_bsel = nc.dram_tensor("bsel", [8, D], F32, kind="ExternalInput").ap()
    d_selw = nc.dram_tensor("selw", [P, 8], F32, kind="ExternalInput").ap()
    d_selwk = nc.dram_tensor("selwk", [8, 1], F32, kind="ExternalInput").ap()
    d_tdec = nc.dram_tensor("tdec", [P, 8], F32, kind="ExternalInput").ap()
    d_tfir = nc.dram_tensor("tfir", [P, 8], F32, kind="ExternalInput").ap()
    d_identb = nc.dram_tensor("identb", [P, P], BF16, kind="ExternalInput").ap()
    d_ones1 = nc.dram_tensor("ones1", [1, P], BF16, kind="ExternalInput").ap()
    d_hmask = nc.dram_tensor("hmask", [P, 1], F32, kind="ExternalInput").ap()
    d_xw = nc.dram_tensor("xw", [TOK, D], F32, kind="ExternalInput").ap()
    d_y = nc.dram_tensor("y", [TOK, D], F32, kind="ExternalOutput").ap()

    es = ExitStack()
    sb = lambda name, shape, dt: es.enter_context(nc.sbuf_tensor(name, shape, dt))

    x_buf = [sb(f"x_buf{i}", [P, 2048], F32) for i in range(3)]
    xn_buf = [sb(f"xn_buf{i}", [P, 2048], BF16) for i in range(2)]
    xnt_buf = [sb(f"xnt_buf{i}", [P, 2048], BF16) for i in range(2)]
    w_bf = {n: sb(f"w_{n}", [P, 8192], BF16) for n in
            ("wk", "wv", "wr", "wo", "wm")}
    wstage = [sb(f"wstage{i}", [P, 1024], F32) for i in range(4)]
    wselbf = [sb(f"wselbf{i}", [P, 1024], BF16) for i in range(2)]
    kv_buf = sb("kv_buf", [P, 2048], F32)
    y_buf = sb("y_buf", [P, 8 * 260], F32)
    r_sb = sb("r_sb", [P, 2048], BF16)
    outr = sb("outr", [P, 2048], BF16)
    wkvf = [sb(f"wkvf{i}", [P, TC], F32) for i in range(2)]
    sg_sb = [sb(f"sg_sb{i}", [P, 512], BF16) for i in range(2)]
    gt_sb = [sb(f"gt_sb{i}", [P, 512], F32) for i in range(4)]
    k_sb = [sb(f"k_sb{i}", [P, TC], F32) for i in range(2)]
    sqscr = sb("sqscr", [P, 1024], BF16)
    s_tdec = sb("s_tdec", [P, 8], F32)
    s_tfir = sb("s_tfir", [P, 8], F32)
    s_expd = sb("s_expd", [P, 8], F32)
    s_wneg = sb("s_wneg", [P, 8], F32)
    s_dec8 = sb("s_dec8", [P, 8], F32)
    s_eu8 = sb("s_eu8", [P, 8], F32)
    s_ident = sb("s_ident", [P, P], BF16)
    s_ones1 = sb("s_ones1", [1, P], BF16)
    s_selw = sb("s_selw", [P, 8], F32)
    s_selwk = sb("s_selwk", [8, 1], F32)
    s_selwkb = sb("s_selwkb", [8, 1], BF16)
    s_bselb = sb("s_bselb", [8, 1024], BF16)
    s_biasrow = sb("s_biasrow", [1, 1024], BF16)
    s_wik = sb("s_wik", [P, 8 * P], BF16)
    s_hmask = sb("s_hmask", [P, 1], F32)
    s_sum = sb("s_sum", [P, 2], F32)
    s_mean = sb("s_mean", [P, 4], F32)
    s_sumsq = sb("s_sumsq", [P, 2], F32)
    s_var = sb("s_var", [P, 4], F32)
    s_msq = sb("s_msq", [P, 2], F32)
    s_lnv = sb("s_lnv", [P, 4], F32)
    s_rstd = sb("s_rstd", [P, 4], F32)
    s_nmr = sb("s_nmr", [P, 4], F32)

    pf = [es.enter_context(nc.psum_tensor(f"pf{i}", [P, 512], F32)) for i in range(6)]
    pb = [es.enter_context(nc.psum_tensor(f"pbt{i}", [P, 1024], BF16)) for i in range(2)]

    sems = {}
    for name in (list(ENGS) + ["qs", "qx0", "qx1", "qx2", "qo0", "qo1"]
                 + [f"qw{i}" for i in range(4)]):
        sems[name] = es.enter_context(nc.semaphore(f"sem_{name}"))

    p = Prog()

    smalls = [
        (s_tdec, d_tdec), (s_tfir, d_tfir),
        (s_ident, d_identb), (s_ones1, d_ones1), (s_selw, d_selw),
        (s_selwk, d_selwk), (s_hmask, d_hmask),
    ]
    for dst, src in smalls:
        p.add("Q", (lambda dst, src: lambda e: e.dma_start(out=dst[:], in_=src))(dst, src),
              reads=(), writes=(f"sb:{dst.name}",), stream="qs")

    p.add("Q", lambda e: e.dma_start(out=wstage[0][0:8, :], in_=d_bsel),
          reads=("wstage0",), writes=("wstage0", "bselw"), stream="qs")
    p.add("V", lambda e: e.tensor_copy(s_bselb[:], wstage[0][0:8, :]),
          reads=("bselw",), writes=("sb:s_bselb", "wstage0"))

    p.add("S", lambda e: e.activation(s_expd[:], s_tdec[:], AF.Exp),
          reads=("sb:s_tdec",), writes=("sb:s_expd",))
    p.add("V", lambda e: e.tensor_scalar(out=s_wneg[:], in0=s_expd[:], scalar1=-1.0,
                                         scalar2=None, op0=ALU.mult),
          reads=("sb:s_expd",), writes=("sb:s_wneg",))
    p.add("S", lambda e: e.activation(s_dec8[:], s_wneg[:], AF.Exp),
          reads=("sb:s_wneg",), writes=("sb:s_dec8",))
    p.add("S", lambda e: e.activation(s_eu8[:], s_tfir[:], AF.Exp),
          reads=("sb:s_tfir",), writes=("sb:s_eu8",))
    p.add("V", lambda e: e.memset(y_buf[:], 0.0), reads=(), writes=("y_init",))

    p.add("V", lambda e: e.tensor_copy(s_selwkb[:], s_selwk[:]),
          reads=("sb:s_selwk",), writes=("sb:s_selwkb",))
    for oc in range(2):
        p.add("T", (lambda oc: lambda e: e.matmul(
            pf[0][0:1, :], s_selwkb[:], s_bselb[:, oc * 512:(oc + 1) * 512],
            start=True, stop=True))(oc),
              reads=("sb:s_selwkb", "sb:s_bselb"), writes=("pf0",))
        p.add("S", (lambda oc: lambda e: e.copy(
            s_biasrow[0:1, oc * 512:(oc + 1) * 512], pf[0][0:1, :]))(oc),
              reads=("pf0",), writes=("sb:s_biasrow",))
    for k in range(8):
        p.add("V", (lambda k: lambda e: e.tensor_scalar(
            out=s_wik[:, k * P:(k + 1) * P], in0=s_ident[:],
            scalar1=s_selw[:, k:k + 1], scalar2=None, op0=ALU.mult))(k),
              reads=("sb:s_ident", "sb:s_selw"), writes=(f"wik{k}",))

    def w_load(dram, dblk, slot):
        return lambda e: e.dma_start(out=wstage[slot][:],
                                     in_=dram[dblk * P:(dblk + 1) * P, :])

    def w_cast(name, dblk, slot):
        return lambda e: e.tensor_copy(
            w_bf[name][:, dblk * 1024:(dblk + 1) * 1024], wstage[slot][:])

    slot_ctr = [0]

    def stream_weight(name, dram):
        for dblk in range(DBLK):
            sl = slot_ctr[0] % 4
            slot_ctr[0] += 1
            p.add("Q", w_load(dram, dblk, sl), reads=(f"wstage{sl}",),
                  writes=(f"wstage{sl}w",), stream=f"qw{sl}")
            p.add("V", w_cast(name, dblk, sl), reads=(f"wstage{sl}w",),
                  writes=(f"wstage{sl}", f"wbf:{name}:{dblk}"))

    def x_load(c, tag):
        buf = x_buf[c % 2]
        return lambda e: e.dma_start(
            out=buf[:].rearrange("p (tb d) -> p tb d", tb=2),
            in_=d_x[c * TC:(c + 1) * TC, :].rearrange("(tb p) d -> p tb d", p=P))

    stream_weight("wk", d_wk)
    stream_weight("wv", d_wv)

    def stream_weight_dblk(name, dram, dblk):
        sl = slot_ctr[0] % 4
        slot_ctr[0] += 1
        p.add("Q", w_load(dram, dblk, sl), reads=(f"wstage{sl}",),
              writes=(f"wstage{sl}w",), stream=f"qw{sl}")
        p.add("V", w_cast(name, dblk, sl), reads=(f"wstage{sl}w",),
              writes=(f"wstage{sl}", f"wbf:{name}:{dblk}"))

    mix_jobs = []
    for dblk in range(DBLK):
        for k in range(8):
            mix_jobs.append((dblk, k))
    mix_ptr = [0]

    def emit_mix_jobs(n):
        for _ in range(n):
            if mix_ptr[0] >= len(mix_jobs):
                return
            dblk, k = mix_jobs[mix_ptr[0]]
            mix_ptr[0] += 1
            sl = slot_ctr[0] % 4
            slot_ctr[0] += 1
            p.add("Q", (lambda dblk, k, sl: lambda e: e.dma_start(
                out=wstage[sl][:],
                in_=d_wsel[k, dblk * P:(dblk + 1) * P, :]))(dblk, k, sl),
                  reads=(f"wstage{sl}",), writes=(f"wstage{sl}w",), stream=f"qw{sl}")
            wsl = mix_ptr[0] % 2
            if mix_ptr[0] % 2 == 0:
                p.add("V", (lambda sl, wsl: lambda e: e.tensor_copy(
                    wselbf[wsl][:], wstage[sl][:]))(sl, wsl),
                      reads=(f"wstage{sl}w",), writes=(f"wstage{sl}", f"wselbf{wsl}"))
            else:
                p.add("S", (lambda sl, wsl: lambda e: e.copy(
                    wselbf[wsl][:], wstage[sl][:]))(sl, wsl),
                      reads=(f"wstage{sl}w",), writes=(f"wstage{sl}", f"wselbf{wsl}"))
            for oc in range(2):
                bank = 4 + oc
                p.add("T", (lambda dblk, k, oc, bank, wsl: lambda e: e.matmul(
                    pf[bank][:], s_wik[:, k * P:(k + 1) * P],
                    wselbf[wsl][:, oc * 512:(oc + 1) * 512],
                    start=(k == 0), stop=(k == 7)))(dblk, k, oc, bank, wsl),
                      reads=(f"wik{k}", f"wselbf{wsl}"), writes=(f"pf{bank}",))
            if k == 7:
                for oc in range(2):
                    bank = 4 + oc
                    p.add("S", (lambda dblk, oc, bank: lambda e: e.copy(
                        w_bf["wm"][:, dblk * 1024 + oc * 512: dblk * 1024 + (oc + 1) * 512],
                        pf[bank][:]))(dblk, oc, bank),
                          reads=(f"pf{bank}",), writes=(f"wbf:wm:{dblk}",))

    def layernorm(c, phase):
        buf = x_buf[c % 3]
        xr = f"x{c % 3}"
        for tb in range(2):
            tile = buf[:, tb * 1024:(tb + 1) * 1024]
            p.add("V", (lambda tile, tb: lambda e: e.tensor_reduce(
                out=s_sum[:, tb:tb + 1], in_=tile, axis=AX.X, op=ALU.add))(tile, tb),
                  reads=(xr,), writes=("lnsum",))
            p.add("S", (lambda c, tile, tb: lambda e: e.activation(
                sqscr[:], tile, AF.Square,
                accum_out=s_sumsq[:, tb:tb + 1]))(c, tile, tb),
                  reads=(xr,), writes=("sqscr", "lnsumsq"))
        pr = c % 2
        p.add("V", (lambda pr: lambda e: e.tensor_scalar(
            out=s_mean[:, 2 * pr:2 * pr + 2], in0=s_sum[:],
            scalar1=1.0 / D, scalar2=None, op0=ALU.mult))(pr),
              reads=("lnsum",), writes=(f"lnmean{pr}",))
        p.add("V", (lambda pr: lambda e: e.tensor_scalar(
            out=s_var[:, 2 * pr:2 * pr + 2], in0=s_sumsq[:],
            scalar1=1.0 / D, scalar2=EPS,
            op0=ALU.mult, op1=ALU.add))(pr),
              reads=("lnsumsq",), writes=(f"lnvar{pr}",))
        p.add("V", (lambda pr: lambda e: e.tensor_tensor(
            out=s_msq[:], in0=s_mean[:, 2 * pr:2 * pr + 2],
            in1=s_mean[:, 2 * pr:2 * pr + 2], op=ALU.mult))(pr),
              reads=(f"lnmean{pr}",), writes=("lnmsq",))
        p.add("V", (lambda pr: lambda e: e.tensor_tensor(
            out=s_var[:, 2 * pr:2 * pr + 2], in0=s_var[:, 2 * pr:2 * pr + 2],
            in1=s_msq[:], op=ALU.subtract))(pr),
              reads=(f"lnvar{pr}", "lnmsq"), writes=(f"lnvar{pr}",))

    def rstd_fin_pair():
        p.add("S", lambda e: e.activation(s_lnv[:], s_var[:], AF.Ln),
              reads=("lnvar0", "lnvar1"), writes=("lnlnv",))
        p.add("S", lambda e: e.activation(
            s_rstd[:], s_lnv[:], AF.Exp, scale=-0.5),
              reads=("lnlnv",), writes=("rstdc0", "rstdc1"))
        p.add("V", lambda e: e.scalar_tensor_tensor(
            out=s_nmr[:], in0=s_mean[:], scalar=-1.0,
            in1=s_rstd[:], op0=ALU.mult, op1=ALU.mult),
              reads=("lnmean0", "lnmean1", "rstdc0", "rstdc1"),
              writes=("nmrc0", "nmrc1"))

    def normalize(c):
        buf = x_buf[c % 3]
        xr = f"x{c % 3}"
        for tb in range(2):
            pr = c % 2
            p.add("S", (lambda c, tb, buf, pr: lambda e: e.activation(
                xn_buf[c % 2][:, tb * 1024:(tb + 1) * 1024],
                buf[:, tb * 1024:(tb + 1) * 1024], AF.Identity,
                bias=s_nmr[:, 2 * pr + tb:2 * pr + tb + 1],
                scale=s_rstd[:, 2 * pr + tb:2 * pr + tb + 1]))(c, tb, buf, pr),
                  reads=(xr, f"rstdc{pr}", f"nmrc{pr}"),
                  writes=(f"xn{c % 2}t{tb}",))

    def transposes(c):
        par = c % 2
        for dblk in range(DBLK):
            bank = dblk % 2
            for tb in range(2):
                p.add("T", (lambda par, dblk, tb, bank: lambda e: e.transpose(
                    pb[bank][:, tb * P:(tb + 1) * P],
                    xn_buf[par][:, tb * 1024 + dblk * P: tb * 1024 + (dblk + 1) * P],
                    s_ident[:]))(par, dblk, tb, bank),
                      reads=(f"xn{par}t{tb}", "sb:s_ident"),
                      writes=(f"pbt{bank}",))
            p.add("S", (lambda par, dblk, bank: lambda e: e.copy(
                xnt_buf[par][:, dblk * TC:(dblk + 1) * TC],
                pb[bank][:, 0:TC]))(par, dblk, bank),
                  reads=(f"pbt{bank}",), writes=(f"xnt{par}d{dblk}",))

    NG = 2 * NCH

    def load_x(g):
        par = g % 3
        dram = d_xw if g < NCH else d_x
        c = g % NCH
        p.add("Q", (lambda dram, c, par: lambda e: e.dma_start(
            out=x_buf[par][:].rearrange("p (tb d) -> p tb d", tb=2),
            in_=dram[c * TC:(c + 1) * TC, :].rearrange("(tb p) d -> p tb d", p=P)))(dram, c, par),
              reads=(f"x{par}",), writes=(f"x{par}",), stream=f"qx{par}")

    def stats(g):
        if g >= NG:
            return
        if g > 1:
            load_x(g)
        layernorm(g, "m")

    def norm1(g):
        if g >= NG:
            return
        normalize(g)

    def kv_mms(g, warm):
        par = g % 2
        for eb in range(CB):
            kbank = 0 + (eb % 2)
            vbank = 2 + (eb % 2)
            for db in range(DBLK):
                p.add("T", (lambda par, eb, db, kbank: lambda e: e.matmul(
                    pf[kbank][:, 0:TC],
                    w_bf["wk"][:, db * 1024 + eb * P: db * 1024 + (eb + 1) * P],
                    xnt_buf[par][:, db * TC:(db + 1) * TC],
                    start=(db == 0), stop=(db == DBLK - 1)))(par, eb, db, kbank),
                      reads=(f"wbf:wk:{db}", f"xnt{par}d{db}"),
                      writes=(f"pf{kbank}",))
            for db in range(DBLK):
                p.add("T", (lambda par, eb, db, vbank: lambda e: e.matmul(
                    pf[vbank][:, 0:TC],
                    w_bf["wv"][:, db * 1024 + eb * P: db * 1024 + (eb + 1) * P],
                    xnt_buf[par][:, db * TC:(db + 1) * TC],
                    start=(db == 0), stop=(db == DBLK - 1)))(par, eb, db, vbank),
                      reads=(f"wbf:wv:{db}", f"xnt{par}d{db}"),
                      writes=(f"pf{vbank}",))
            ksl = eb % 2
            if warm:
                p.add("S", (lambda eb, kbank, ksl: lambda e: e.activation(
                    k_sb[ksl][:], pf[kbank][:, 0:TC], AF.Identity,
                    scale=s_hmask[:, 0:1]))(eb, kbank, ksl),
                      reads=(f"pf{kbank}", "sb:s_hmask"), writes=(f"ksb{ksl}",))
            else:
                p.add("S", (lambda eb, kbank, ksl: lambda e: e.copy(
                    k_sb[ksl][:], pf[kbank][:, 0:TC]))(eb, kbank, ksl),
                      reads=(f"pf{kbank}",), writes=(f"ksb{ksl}",))
            p.add("V", (lambda eb, vbank, ksl: lambda e: e.tensor_tensor(
                out=kv_buf[:, eb * TC:(eb + 1) * TC], in0=k_sb[ksl][:],
                in1=pf[vbank][:, 0:TC], op=ALU.mult))(eb, vbank, ksl),
                  reads=(f"ksb{ksl}", f"pf{vbank}"), writes=(f"kv{eb}",))

    def scans(g, with_wkv):
        for cb in range(CB):
            ybase = cb * 260
            if g > 0:
                p.add("V", (lambda cb, ybase: lambda e: e.tensor_copy(
                    y_buf[:, ybase:ybase + 1], y_buf[:, ybase + TC:ybase + TC + 1]))(cb, ybase),
                      reads=(f"y{cb}",), writes=(f"y{cb}c",))
            init = 0.0 if g == 0 else None
            p.add("V", (lambda cb, ybase, init: lambda e: e.tensor_tensor_scan(
                out=y_buf[:, ybase + 1:ybase + 1 + TC],
                data0=s_dec8[:, cb:cb + 1].broadcast_to((P, TC)),
                data1=kv_buf[:, cb * TC:(cb + 1) * TC],
                initial=(y_buf[:, ybase:ybase + 1] if init is None else 0.0),
                op0=ALU.mult, op1=ALU.add))(cb, ybase, init),
                  reads=("sb:s_dec8", f"kv{cb}", f"y{cb}c", "y_init"),
                  writes=(f"y{cb}",))
            if with_wkv:
                sl = cb % 2
                p.add("V", (lambda cb, sl: lambda e: e.tensor_scalar(
                    out=wkvf[sl][:], in0=kv_buf[:, cb * TC:(cb + 1) * TC],
                    scalar1=s_eu8[:, cb:cb + 1], scalar2=None, op0=ALU.mult))(cb, sl),
                      reads=(f"kv{cb}", "sb:s_eu8"), writes=(f"wkvf{sl}",))
                p.add("V", (lambda cb, ybase, sl: lambda e: e.tensor_tensor(
                    out=wkvf[sl][:], in0=wkvf[sl][:],
                    in1=y_buf[:, ybase:ybase + TC], op=ALU.add))(cb, ybase, sl),
                      reads=(f"wkvf{sl}", f"y{cb}", f"y{cb}c", "y_init"),
                      writes=(f"wkvf{sl}",))
                p.add("V", (lambda cb, sl: lambda e: e.tensor_tensor(
                    out=outr[:, cb * TC:(cb + 1) * TC], in0=wkvf[sl][:],
                    in1=r_sb[:, cb * TC:(cb + 1) * TC], op=ALU.mult))(cb, sl),
                      reads=(f"wkvf{sl}", "rsball"), writes=("outr",))

    def r_mms(g):
        par = g % 2
        for eb in range(CB):
            rbank = 0 + (eb % 2)
            for db in range(DBLK):
                p.add("T", (lambda par, eb, db, rbank: lambda e: e.matmul(
                    pf[rbank][:, 0:TC],
                    w_bf["wr"][:, db * 1024 + eb * P: db * 1024 + (eb + 1) * P],
                    xnt_buf[par][:, db * TC:(db + 1) * TC],
                    start=(db == 0), stop=(db == DBLK - 1)))(par, eb, db, rbank),
                      reads=(f"wbf:wr:{db}", f"xnt{par}d{db}"),
                      writes=(f"pf{rbank}",))
            p.add("S", (lambda eb, rbank: lambda e: e.activation(
                r_sb[:, eb * TC:(eb + 1) * TC], pf[rbank][:, 0:TC],
                AF.Sigmoid))(eb, rbank),
                  reads=(f"pf{rbank}",), writes=("rsball",))

    def quadrants_ffn(g):
        par = g % 2
        for q, (tb, oc) in enumerate([(0, 0), (0, 1), (1, 0), (1, 1)]):
            fbank = 4 + oc
            for db in range(DBLK):
                p.add("T", (lambda par, tb, oc, db, fbank: lambda e: e.matmul(
                    pf[fbank][:],
                    xnt_buf[par][:, db * TC + tb * P: db * TC + (tb + 1) * P],
                    w_bf["wm"][:, db * 1024 + oc * 512: db * 1024 + (oc + 1) * 512],
                    start=(db == 0), stop=False))(par, tb, oc, db, fbank),
                      reads=(f"xnt{par}d{db}", f"wbf:wm:{db}"),
                      writes=(f"pf{fbank}",))
            p.add("T", (lambda tb, oc, fbank: lambda e: e.matmul(
                pf[fbank][:], s_ones1[:],
                s_biasrow[0:1, oc * 512:(oc + 1) * 512],
                start=False, stop=True))(tb, oc, fbank),
                  reads=("sb:s_ones1", "sb:s_biasrow"),
                  writes=(f"pf{fbank}",))
            sl = oc
            p.add("S", (lambda fbank, sl: lambda e: e.activation(
                sg_sb[sl][:], pf[fbank][:], AF.Sigmoid))(fbank, sl),
                  reads=(f"pf{fbank}",), writes=(f"sg{sl}",))
            p.add("V", (lambda fbank, sl, q: lambda e: e.tensor_tensor(
                out=gt_sb[q][:], in0=pf[fbank][:], in1=sg_sb[sl][:],
                op=ALU.mult))(fbank, sl, q),
                  reads=(f"pf{fbank}", f"sg{sl}"), writes=(f"gt{q}",))
    def quadrants_wo(g):
        par = g % 2
        xpar = g % 3
        c = g - NCH
        for q, (tb, oc) in enumerate([(0, 0), (0, 1), (1, 0), (1, 1)]):
            bank = 2 + oc
            for eb in range(CB):
                p.add("T", (lambda par, tb, oc, eb, bank: lambda e: e.matmul(
                    pf[bank][:],
                    outr[:, eb * TC + tb * P: eb * TC + (tb + 1) * P],
                    w_bf["wo"][:, eb * 1024 + oc * 512: eb * 1024 + (oc + 1) * 512],
                    start=(eb == 0), stop=(eb == CB - 1)))(par, tb, oc, eb, bank),
                      reads=("outr", f"wbf:wo:{eb}"),
                      writes=(f"pf{bank}",))
            p.add("V", (lambda bank, q: lambda e: e.tensor_tensor(
                out=gt_sb[q][:], in0=pf[bank][:], in1=gt_sb[q][:],
                op=ALU.mult))(bank, q),
                  reads=(f"pf{bank}", f"gt{q}"), writes=(f"gt{q}",))
            p.add("V", (lambda xpar, tb, oc, q: lambda e: e.tensor_tensor(
                out=x_buf[xpar][:, tb * 1024 + oc * 512: tb * 1024 + (oc + 1) * 512],
                in0=gt_sb[q][:],
                in1=x_buf[xpar][:, tb * 1024 + oc * 512: tb * 1024 + (oc + 1) * 512],
                op=ALU.add))(xpar, tb, oc, q),
                  reads=(f"gt{q}", f"x{xpar}"), writes=(f"x{xpar}",))
        p.add("Q", (lambda c, par, xpar: lambda e: e.dma_start(
            out=d_y[c * TC:(c + 1) * TC, :].rearrange("(tb p) d -> p tb d", p=P),
            in_=x_buf[xpar][:].rearrange("p (tb d) -> p tb d", tb=2)))(c, par, xpar),
              reads=(f"x{xpar}",), writes=(f"out{c}",), stream=f"qo{par}")

    def back_warm(g):
        transposes(g)
        kv_mms(g, True)
        stream_weight_dblk("wr", d_wr, g)
        stream_weight_dblk("wo", d_wo, g)
        emit_mix_jobs(8)
        scans(g, with_wkv=False)

    def head_main(g):
        transposes(g)
        kv_mms(g, False)
        r_mms(g)

    load_x(0)
    load_x(1)
    stats(0)
    stats(1)
    rstd_fin_pair()
    norm1(0)
    norm1(1)
    for g in range(NCH):
        stats(g + 2)
        if (g + 2) % 2 == 1:
            rstd_fin_pair()
        norm1(g + 1) if g + 1 >= 2 else None
        back_warm(g)
    head_main(NCH)
    for g in range(NCH, NG):
        stats(g + 2)
        quadrants_ffn(g)
        if g + 2 < NG and (g + 2) % 2 == 1:
            rstd_fin_pair()
        norm1(g + 1)
        scans(g, with_wkv=True)
        if g + 1 < NG:
            head_main(g + 1)
        quadrants_wo(g)

    p.add("G", lambda e: e.engine_nop(),
          reads=tuple(f"out{c}" for c in range(NCH)), writes=())

    p.finalize()

    with nc.Block() as block:
        @block.tensor
        def _(eng):
            p.emit_engine("T", eng, sems)

        @block.vector
        def _(eng):
            p.emit_engine("V", eng, sems)

        @block.scalar
        def _(eng):
            p.emit_engine("S", eng, sems)

        @block.sync
        def _(eng):
            p.emit_engine("Q", eng, sems)

        @block.gpsimd
        def _(eng):
            p.emit_engine("G", eng, sems)

    es.close()
    return nc


_NC_CACHE = None


def kernel(L0_attn_x, L0_attn_Wk, L0_attn_Wv, L0_attn_Wr, L0_attn_Wo,
           L0_attn_time_decay, L0_attn_time_first,
           L0_ffn_W, L0_ffn_b, L0_ffn_ids, L0_ffn_weights):
    global _NC_CACHE
    x = np.asarray(L0_attn_x, np.float32)
    wkT = np.ascontiguousarray(np.asarray(L0_attn_Wk, np.float32).T)
    wvT = np.ascontiguousarray(np.asarray(L0_attn_Wv, np.float32).T)
    wrT = np.ascontiguousarray(np.asarray(L0_attn_Wr, np.float32).T)
    woT = np.ascontiguousarray(np.asarray(L0_attn_Wo, np.float32).T)
    ids = np.asarray(L0_ffn_ids).astype(np.int64)
    wselT = np.ascontiguousarray(
        np.asarray(L0_ffn_W, np.float32)[ids].transpose(0, 2, 1))
    bsel = np.ascontiguousarray(np.asarray(L0_ffn_b, np.float32)[ids])
    selw = np.ascontiguousarray(
        np.broadcast_to(np.asarray(L0_ffn_weights, np.float32), (P, 8)))
    selwk = np.ascontiguousarray(
        np.asarray(L0_ffn_weights, np.float32).reshape(8, 1))
    tdec = np.ascontiguousarray(
        np.asarray(L0_attn_time_decay, np.float32).reshape(8, P).T)
    tfir = np.ascontiguousarray(
        np.asarray(L0_attn_time_first, np.float32).reshape(8, P).T)
    identb = np.eye(P, dtype=np.float32).astype(ml_dtypes.bfloat16)
    ones1 = np.ones((1, P), dtype=ml_dtypes.bfloat16)

    if _NC_CACHE is None:
        _NC_CACHE = build_program()
    nc = _NC_CACHE

    in_maps = []
    for c in range(NCORE):
        b, h = c // 2, c % 2
        in_maps.append({
            "x": np.ascontiguousarray(x[b, h * TOK:(h + 1) * TOK, :]),
            "xw": np.ascontiguousarray(x[b, 0:TOK, :]),
            "wk": wkT, "wv": wvT, "wr": wrT, "wo": woT,
            "wsel": wselT, "bsel": bsel, "selw": selw, "selwk": selwk,
            "tdec": tdec, "tfir": tfir,
            "identb": identb, "ones1": ones1,
            "hmask": np.full((P, 1), float(h), np.float32),
        })

    res = run_bass_kernel_spmd(nc, in_maps, core_ids=list(range(NCORE)))
    out = np.empty((4, 4096, D), np.float32)
    for c in range(NCORE):
        b, h = c // 2, c % 2
        out[b, h * TOK:(h + 1) * TOK, :] = np.asarray(res.results[c]["y"])
    return out


# version 31
# speedup vs baseline: 1.0421x; 1.0042x over previous
import numpy as np
import ml_dtypes
from contextlib import ExitStack

import concourse.bass as bass
import concourse.mybir as mybir
from concourse.bass_utils import run_bass_kernel_spmd

F32 = mybir.dt.float32
BF16 = mybir.dt.bfloat16
U32 = mybir.dt.uint32
AF = mybir.ActivationFunctionType
ALU = mybir.AluOpType
AX = mybir.AxisListType

NCORE = 8
P = 128
D = 1024
TOK = 2048
TC = 256
NCH = TOK // TC
CB = 8
DBLK = 8
EPS = 1e-5

ENGS = ("T", "V", "S", "Q", "G")

_DETECTOR_SAFE = False


class _Op:
    __slots__ = ("eng", "emit", "reads", "writes", "deps", "inc", "idx",
                 "val", "waits", "custom_sem", "amount", "stream", "issue_val")

    def __init__(self, eng, emit, reads, writes, custom_sem=None):
        self.eng = eng
        self.stream = None
        self.emit = emit
        self.reads = tuple(reads)
        self.writes = tuple(writes)
        self.custom_sem = custom_sem
        self.deps = None
        self.inc = False
        self.val = None
        self.waits = None
        self.amount = 16 if eng in ("Q",) else 1


class Prog:
    def __init__(self):
        self.ops = []

    def add(self, eng, emit, reads=(), writes=(), custom_sem=None, amount=None,
            stream=None):
        op = _Op(eng, emit, reads, writes, custom_sem)
        if amount is not None:
            op.amount = amount
        if eng == "Q":
            op.stream = stream or "qs"
        op.idx = len(self.ops)
        self.ops.append(op)
        return op

    def finalize(self):
        last_writer = {}
        readers = {}
        for op in self.ops:
            deps = set()
            for r in op.reads:
                w = last_writer.get(r)
                if w is not None and w is not op:
                    deps.add(w)
            for w in op.writes:
                pw = last_writer.get(w)
                if pw is not None and pw is not op:
                    deps.add(pw)
                for rd in readers.get(w, ()):
                    if rd is not op:
                        deps.add(rd)
            for r in op.reads:
                readers.setdefault(r, []).append(op)
            for w in op.writes:
                last_writer[w] = op
                readers[w] = []
            op.deps = [d for d in deps
                       if not (d.eng == op.eng and op.eng == "T")
                       and not (op.eng == "Q" and d.eng == "Q"
                                and d.stream == op.stream)]
        for op in self.ops:
            for d in op.deps:
                if d.custom_sem is None:
                    d.inc = True
        counters = {e: 0 for e in ENGS}
        qcounters = {}
        for op in self.ops:
            if op.eng == "Q":
                op.inc = True
                qcounters[op.stream] = qcounters.get(op.stream, 0) + op.amount
                op.val = qcounters[op.stream]
                op.issue_val = op.val
            elif op.custom_sem is not None:
                op.val = None
            elif op.inc:
                counters[op.eng] += op.amount
                op.val = counters[op.eng]
            else:
                op.val = None
        qs_total = qcounters.get("qs", 0)
        for op in self.ops:
            if op.eng == "Q" and op.stream == "qs":
                op.val = qs_total
        marks = {e: {f: 0 for f in ENGS} for e in ENGS}
        cmarks = {e: {} for e in ENGS}
        for op in self.ops:
            waits = []
            need = {}
            cneed = {}
            for d in op.deps:
                if d.custom_sem is not None:
                    nm, v = d.custom_sem
                    cneed[nm] = max(cneed.get(nm, 0), v)
                elif d.eng == "Q":
                    cneed[d.stream] = max(cneed.get(d.stream, 0), d.val)
                else:
                    assert d.val is not None
                    need[d.eng] = max(need.get(d.eng, 0), d.val)
            for e, v in sorted(need.items()):
                if v > marks[op.eng][e]:
                    waits.append((e, v))
                    marks[op.eng][e] = v
            for nm, v in sorted(cneed.items()):
                if v > cmarks[op.eng].get(nm, 0):
                    waits.append((nm, v))
                    cmarks[op.eng][nm] = v
            op.waits = waits

    def emit_engine(self, eng_name, eng, sems):
        for op in self.ops:
            if op.eng != eng_name:
                continue
            for tgt, v in op.waits:
                eng.wait_ge(sems[tgt], v)
            if _DETECTOR_SAFE and op.eng == "Q" and op.issue_val > op.amount:
                eng.wait_ge(sems[op.stream], op.issue_val - op.amount)
            inst = op.emit(eng)
            if op.eng == "Q":
                inst.then_inc(sems[op.stream], op.amount)
            elif op.inc and op.custom_sem is None:
                assert inst is not None, f"op {op.idx} on {eng_name} needs inc"
                inst.then_inc(sems[op.eng], op.amount)


def build_program():
    nc = bass.Bass("TRN2", target_bir_lowering=False,
                   detect_race_conditions=False)

    d_x = nc.dram_tensor("x", [TOK, D], F32, kind="ExternalInput").ap()
    d_wk = nc.dram_tensor("wk", [D, D], F32, kind="ExternalInput").ap()
    d_wv = nc.dram_tensor("wv", [D, D], F32, kind="ExternalInput").ap()
    d_wr = nc.dram_tensor("wr", [D, D], F32, kind="ExternalInput").ap()
    d_wo = nc.dram_tensor("wo", [D, D], F32, kind="ExternalInput").ap()
    d_wsel = nc.dram_tensor("wsel", [8, D, D], F32, kind="ExternalInput").ap()
    d_bsel = nc.dram_tensor("bsel", [8, D], F32, kind="ExternalInput").ap()
    d_selw = nc.dram_tensor("selw", [P, 8], F32, kind="ExternalInput").ap()
    d_selwk = nc.dram_tensor("selwk", [8, 1], F32, kind="ExternalInput").ap()
    d_tdec = nc.dram_tensor("tdec", [P, 8], F32, kind="ExternalInput").ap()
    d_tfir = nc.dram_tensor("tfir", [P, 8], F32, kind="ExternalInput").ap()
    d_identb = nc.dram_tensor("identb", [P, P], BF16, kind="ExternalInput").ap()
    d_ones1 = nc.dram_tensor("ones1", [1, P], BF16, kind="ExternalInput").ap()
    d_hmask = nc.dram_tensor("hmask", [P, 1], F32, kind="ExternalInput").ap()
    d_xw = nc.dram_tensor("xw", [TOK, D], F32, kind="ExternalInput").ap()
    d_y = nc.dram_tensor("y", [TOK, D], F32, kind="ExternalOutput").ap()

    es = ExitStack()
    sb = lambda name, shape, dt: es.enter_context(nc.sbuf_tensor(name, shape, dt))

    x_buf = [sb(f"x_buf{i}", [P, 2048], F32) for i in range(3)]
    xn_buf = [sb(f"xn_buf{i}", [P, 2048], BF16) for i in range(2)]
    xnt_buf = [sb(f"xnt_buf{i}", [P, 2048], BF16) for i in range(2)]
    w_bf = {n: sb(f"w_{n}", [P, 8192], BF16) for n in
            ("wk", "wv", "wr", "wo", "wm")}
    wstage = [sb(f"wstage{i}", [P, 1024], F32) for i in range(4)]
    wselbf = [sb(f"wselbf{i}", [P, 1024], BF16) for i in range(2)]
    kv_buf = sb("kv_buf", [P, 2048], F32)
    y_buf = sb("y_buf", [P, 8 * 260], F32)
    r_sb = sb("r_sb", [P, 2048], BF16)
    outr = sb("outr", [P, 2048], BF16)
    wkvf = [sb(f"wkvf{i}", [P, TC], F32) for i in range(2)]
    sg_sb = [sb(f"sg_sb{i}", [P, 512], BF16) for i in range(2)]
    gt_sb = [sb(f"gt_sb{i}", [P, 512], F32) for i in range(4)]
    k_sb = [sb(f"k_sb{i}", [P, TC], F32) for i in range(2)]
    sqscr = sb("sqscr", [P, 1024], BF16)
    s_tdec = sb("s_tdec", [P, 8], F32)
    s_tfir = sb("s_tfir", [P, 8], F32)
    s_expd = sb("s_expd", [P, 8], F32)
    s_wneg = sb("s_wneg", [P, 8], F32)
    s_dec8 = sb("s_dec8", [P, 8], F32)
    s_eu8 = sb("s_eu8", [P, 8], F32)
    s_ident = sb("s_ident", [P, P], BF16)
    s_ones1 = sb("s_ones1", [1, P], BF16)
    s_selw = sb("s_selw", [P, 8], F32)
    s_selwk = sb("s_selwk", [8, 1], F32)
    s_selwkb = sb("s_selwkb", [8, 1], BF16)
    s_bselb = sb("s_bselb", [8, 1024], BF16)
    s_biasrow = sb("s_biasrow", [1, 1024], BF16)
    s_wik = sb("s_wik", [P, 8 * P], BF16)
    s_hmask = sb("s_hmask", [P, 1], F32)
    s_sum = sb("s_sum", [P, 2], F32)
    s_mean = sb("s_mean", [P, 4], F32)
    s_sumsq = sb("s_sumsq", [P, 2], F32)
    s_var = sb("s_var", [P, 4], F32)
    s_msq = sb("s_msq", [P, 2], F32)
    s_lnv = sb("s_lnv", [P, 4], F32)
    s_rstd = sb("s_rstd", [P, 4], F32)
    s_nmr = sb("s_nmr", [P, 4], F32)

    pf = [es.enter_context(nc.psum_tensor(f"pf{i}", [P, 512], F32)) for i in range(6)]
    pb = [es.enter_context(nc.psum_tensor(f"pbt{i}", [P, 1024], BF16)) for i in range(2)]

    sems = {}
    for name in (list(ENGS) + ["qs", "qx0", "qx1", "qx2", "qo0", "qo1"]
                 + [f"qw{i}" for i in range(4)]):
        sems[name] = es.enter_context(nc.semaphore(f"sem_{name}"))

    p = Prog()

    smalls = [
        (s_tdec, d_tdec), (s_tfir, d_tfir),
        (s_ident, d_identb), (s_ones1, d_ones1), (s_selw, d_selw),
        (s_selwk, d_selwk), (s_hmask, d_hmask),
    ]
    for dst, src in smalls:
        p.add("Q", (lambda dst, src: lambda e: e.dma_start(out=dst[:], in_=src))(dst, src),
              reads=(), writes=(f"sb:{dst.name}",), stream="qs")

    p.add("Q", lambda e: e.dma_start(out=wstage[0][0:8, :], in_=d_bsel),
          reads=("wstage0",), writes=("wstage0", "bselw"), stream="qs")
    p.add("V", lambda e: e.tensor_copy(s_bselb[:], wstage[0][0:8, :]),
          reads=("bselw",), writes=("sb:s_bselb", "wstage0"))

    p.add("S", lambda e: e.activation(s_expd[:], s_tdec[:], AF.Exp),
          reads=("sb:s_tdec",), writes=("sb:s_expd",))
    p.add("V", lambda e: e.tensor_scalar(out=s_wneg[:], in0=s_expd[:], scalar1=-1.0,
                                         scalar2=None, op0=ALU.mult),
          reads=("sb:s_expd",), writes=("sb:s_wneg",))
    p.add("S", lambda e: e.activation(s_dec8[:], s_wneg[:], AF.Exp),
          reads=("sb:s_wneg",), writes=("sb:s_dec8",))
    p.add("S", lambda e: e.activation(s_eu8[:], s_tfir[:], AF.Exp),
          reads=("sb:s_tfir",), writes=("sb:s_eu8",))
    p.add("V", lambda e: e.memset(y_buf[:], 0.0), reads=(), writes=("y_init",))

    p.add("V", lambda e: e.tensor_copy(s_selwkb[:], s_selwk[:]),
          reads=("sb:s_selwk",), writes=("sb:s_selwkb",))
    for oc in range(2):
        p.add("T", (lambda oc: lambda e: e.matmul(
            pf[0][0:1, :], s_selwkb[:], s_bselb[:, oc * 512:(oc + 1) * 512],
            start=True, stop=True))(oc),
              reads=("sb:s_selwkb", "sb:s_bselb"), writes=("pf0",))
        p.add("S", (lambda oc: lambda e: e.copy(
            s_biasrow[0:1, oc * 512:(oc + 1) * 512], pf[0][0:1, :]))(oc),
              reads=("pf0",), writes=("sb:s_biasrow",))
    for k in range(8):
        p.add("V", (lambda k: lambda e: e.tensor_scalar(
            out=s_wik[:, k * P:(k + 1) * P], in0=s_ident[:],
            scalar1=s_selw[:, k:k + 1], scalar2=None, op0=ALU.mult))(k),
              reads=("sb:s_ident", "sb:s_selw"), writes=(f"wik{k}",))

    def w_load(dram, dblk, slot):
        return lambda e: e.dma_start(out=wstage[slot][:],
                                     in_=dram[dblk * P:(dblk + 1) * P, :])

    def w_cast(name, dblk, slot):
        return lambda e: e.tensor_copy(
            w_bf[name][:, dblk * 1024:(dblk + 1) * 1024], wstage[slot][:])

    slot_ctr = [0]

    def stream_weight(name, dram):
        for dblk in range(DBLK):
            sl = slot_ctr[0] % 4
            slot_ctr[0] += 1
            p.add("Q", w_load(dram, dblk, sl), reads=(f"wstage{sl}",),
                  writes=(f"wstage{sl}w",), stream=f"qw{sl}")
            p.add("V", w_cast(name, dblk, sl), reads=(f"wstage{sl}w",),
                  writes=(f"wstage{sl}", f"wbf:{name}:{dblk}"))

    def x_load(c, tag):
        buf = x_buf[c % 2]
        return lambda e: e.dma_start(
            out=buf[:].rearrange("p (tb d) -> p tb d", tb=2),
            in_=d_x[c * TC:(c + 1) * TC, :].rearrange("(tb p) d -> p tb d", p=P))

    def stream_weight_dblk(name, dram, dblk):
        sl = slot_ctr[0] % 4
        slot_ctr[0] += 1
        p.add("Q", w_load(dram, dblk, sl), reads=(f"wstage{sl}",),
              writes=(f"wstage{sl}w",), stream=f"qw{sl}")
        p.add("V", w_cast(name, dblk, sl), reads=(f"wstage{sl}w",),
              writes=(f"wstage{sl}", f"wbf:{name}:{dblk}"))

    mix_jobs = []
    for dblk in range(DBLK):
        for k in range(8):
            mix_jobs.append((dblk, k))
    mix_ptr = [0]

    def emit_mix_jobs(n):
        for _ in range(n):
            if mix_ptr[0] >= len(mix_jobs):
                return
            dblk, k = mix_jobs[mix_ptr[0]]
            mix_ptr[0] += 1
            sl = slot_ctr[0] % 4
            slot_ctr[0] += 1
            p.add("Q", (lambda dblk, k, sl: lambda e: e.dma_start(
                out=wstage[sl][:],
                in_=d_wsel[k, dblk * P:(dblk + 1) * P, :]))(dblk, k, sl),
                  reads=(f"wstage{sl}",), writes=(f"wstage{sl}w",), stream=f"qw{sl}")
            wsl = mix_ptr[0] % 2
            if mix_ptr[0] % 2 == 0:
                p.add("V", (lambda sl, wsl: lambda e: e.tensor_copy(
                    wselbf[wsl][:], wstage[sl][:]))(sl, wsl),
                      reads=(f"wstage{sl}w",), writes=(f"wstage{sl}", f"wselbf{wsl}"))
            else:
                p.add("S", (lambda sl, wsl: lambda e: e.copy(
                    wselbf[wsl][:], wstage[sl][:]))(sl, wsl),
                      reads=(f"wstage{sl}w",), writes=(f"wstage{sl}", f"wselbf{wsl}"))
            for oc in range(2):
                bank = 4 + oc
                p.add("T", (lambda dblk, k, oc, bank, wsl: lambda e: e.matmul(
                    pf[bank][:], s_wik[:, k * P:(k + 1) * P],
                    wselbf[wsl][:, oc * 512:(oc + 1) * 512],
                    start=(k == 0), stop=(k == 7)))(dblk, k, oc, bank, wsl),
                      reads=(f"wik{k}", f"wselbf{wsl}"), writes=(f"pf{bank}",))
            if k == 7:
                for oc in range(2):
                    bank = 4 + oc
                    p.add("S", (lambda dblk, oc, bank: lambda e: e.copy(
                        w_bf["wm"][:, dblk * 1024 + oc * 512: dblk * 1024 + (oc + 1) * 512],
                        pf[bank][:]))(dblk, oc, bank),
                          reads=(f"pf{bank}",), writes=(f"wbf:wm:{dblk}",))

    def layernorm(c, phase):
        buf = x_buf[c % 3]
        xr = f"x{c % 3}"
        for tb in range(2):
            tile = buf[:, tb * 1024:(tb + 1) * 1024]
            p.add("V", (lambda tile, tb: lambda e: e.tensor_reduce(
                out=s_sum[:, tb:tb + 1], in_=tile, axis=AX.X, op=ALU.add))(tile, tb),
                  reads=(xr,), writes=("lnsum",))
            p.add("S", (lambda c, tile, tb: lambda e: e.activation(
                sqscr[:], tile, AF.Square,
                accum_out=s_sumsq[:, tb:tb + 1]))(c, tile, tb),
                  reads=(xr,), writes=("sqscr", "lnsumsq"))
        pr = c % 2
        p.add("V", (lambda pr: lambda e: e.tensor_scalar(
            out=s_mean[:, 2 * pr:2 * pr + 2], in0=s_sum[:],
            scalar1=1.0 / D, scalar2=None, op0=ALU.mult))(pr),
              reads=("lnsum",), writes=(f"lnmean{pr}",))
        p.add("V", (lambda pr: lambda e: e.tensor_scalar(
            out=s_var[:, 2 * pr:2 * pr + 2], in0=s_sumsq[:],
            scalar1=1.0 / D, scalar2=EPS,
            op0=ALU.mult, op1=ALU.add))(pr),
              reads=("lnsumsq",), writes=(f"lnvar{pr}",))
        p.add("V", (lambda pr: lambda e: e.tensor_tensor(
            out=s_msq[:], in0=s_mean[:, 2 * pr:2 * pr + 2],
            in1=s_mean[:, 2 * pr:2 * pr + 2], op=ALU.mult))(pr),
              reads=(f"lnmean{pr}",), writes=("lnmsq",))
        p.add("V", (lambda pr: lambda e: e.tensor_tensor(
            out=s_var[:, 2 * pr:2 * pr + 2], in0=s_var[:, 2 * pr:2 * pr + 2],
            in1=s_msq[:], op=ALU.subtract))(pr),
              reads=(f"lnvar{pr}", "lnmsq"), writes=(f"lnvar{pr}",))

    def rstd_fin_pair():
        p.add("S", lambda e: e.activation(s_lnv[:], s_var[:], AF.Ln),
              reads=("lnvar0", "lnvar1"), writes=("lnlnv",))
        p.add("S", lambda e: e.activation(
            s_rstd[:], s_lnv[:], AF.Exp, scale=-0.5),
              reads=("lnlnv",), writes=("rstdc0", "rstdc1"))
        p.add("V", lambda e: e.scalar_tensor_tensor(
            out=s_nmr[:], in0=s_mean[:], scalar=-1.0,
            in1=s_rstd[:], op0=ALU.mult, op1=ALU.mult),
              reads=("lnmean0", "lnmean1", "rstdc0", "rstdc1"),
              writes=("nmrc0", "nmrc1"))

    def normalize(c):
        buf = x_buf[c % 3]
        xr = f"x{c % 3}"
        for tb in range(2):
            pr = c % 2
            p.add("S", (lambda c, tb, buf, pr: lambda e: e.activation(
                xn_buf[c % 2][:, tb * 1024:(tb + 1) * 1024],
                buf[:, tb * 1024:(tb + 1) * 1024], AF.Identity,
                bias=s_nmr[:, 2 * pr + tb:2 * pr + tb + 1],
                scale=s_rstd[:, 2 * pr + tb:2 * pr + tb + 1]))(c, tb, buf, pr),
                  reads=(xr, f"rstdc{pr}", f"nmrc{pr}"),
                  writes=(f"xn{c % 2}t{tb}",))

    def transposes(c):
        par = c % 2
        for dblk in range(DBLK):
            bank = dblk % 2
            for tb in range(2):
                p.add("T", (lambda par, dblk, tb, bank: lambda e: e.transpose(
                    pb[bank][:, tb * P:(tb + 1) * P],
                    xn_buf[par][:, tb * 1024 + dblk * P: tb * 1024 + (dblk + 1) * P],
                    s_ident[:]))(par, dblk, tb, bank),
                      reads=(f"xn{par}t{tb}", "sb:s_ident"),
                      writes=(f"pbt{bank}",))
            p.add("S", (lambda par, dblk, bank: lambda e: e.copy(
                xnt_buf[par][:, dblk * TC:(dblk + 1) * TC],
                pb[bank][:, 0:TC]))(par, dblk, bank),
                  reads=(f"pbt{bank}",), writes=(f"xnt{par}d{dblk}",))

    NG = 2 * NCH

    def load_x(g):
        par = g % 3
        dram = d_xw if g < NCH else d_x
        c = g % NCH
        p.add("Q", (lambda dram, c, par: lambda e: e.dma_start(
            out=x_buf[par][:].rearrange("p (tb d) -> p tb d", tb=2),
            in_=dram[c * TC:(c + 1) * TC, :].rearrange("(tb p) d -> p tb d", p=P)))(dram, c, par),
              reads=(f"x{par}",), writes=(f"x{par}",), stream=f"qx{par}")

    def stats(g):
        if g >= NG:
            return
        if g > 1:
            load_x(g)
        layernorm(g, "m")

    def norm1(g):
        if g >= NG:
            return
        normalize(g)

    def kv_mms(g, warm):
        par = g % 2
        for eb in range(CB):
            kbank = 0 + (eb % 2)
            vbank = 2 + (eb % 2)
            for db in range(DBLK):
                p.add("T", (lambda par, eb, db, kbank: lambda e: e.matmul(
                    pf[kbank][:, 0:TC],
                    w_bf["wk"][:, db * 1024 + eb * P: db * 1024 + (eb + 1) * P],
                    xnt_buf[par][:, db * TC:(db + 1) * TC],
                    start=(db == 0), stop=(db == DBLK - 1)))(par, eb, db, kbank),
                      reads=(f"wbf:wk:{db}", f"xnt{par}d{db}"),
                      writes=(f"pf{kbank}",))
            for db in range(DBLK):
                p.add("T", (lambda par, eb, db, vbank: lambda e: e.matmul(
                    pf[vbank][:, 0:TC],
                    w_bf["wv"][:, db * 1024 + eb * P: db * 1024 + (eb + 1) * P],
                    xnt_buf[par][:, db * TC:(db + 1) * TC],
                    start=(db == 0), stop=(db == DBLK - 1)))(par, eb, db, vbank),
                      reads=(f"wbf:wv:{db}", f"xnt{par}d{db}"),
                      writes=(f"pf{vbank}",))
            ksl = eb % 2
            if warm:
                p.add("S", (lambda eb, kbank, ksl: lambda e: e.activation(
                    k_sb[ksl][:], pf[kbank][:, 0:TC], AF.Identity,
                    scale=s_hmask[:, 0:1]))(eb, kbank, ksl),
                      reads=(f"pf{kbank}", "sb:s_hmask"), writes=(f"ksb{ksl}",))
            else:
                p.add("S", (lambda eb, kbank, ksl: lambda e: e.copy(
                    k_sb[ksl][:], pf[kbank][:, 0:TC]))(eb, kbank, ksl),
                      reads=(f"pf{kbank}",), writes=(f"ksb{ksl}",))
            p.add("V", (lambda eb, vbank, ksl: lambda e: e.tensor_tensor(
                out=kv_buf[:, eb * TC:(eb + 1) * TC], in0=k_sb[ksl][:],
                in1=pf[vbank][:, 0:TC], op=ALU.mult))(eb, vbank, ksl),
                  reads=(f"ksb{ksl}", f"pf{vbank}"), writes=(f"kv{eb}",))

    def scans(g, with_wkv):
        for cb in range(CB):
            ybase = cb * 260
            if g > 0:
                p.add("V", (lambda cb, ybase: lambda e: e.tensor_copy(
                    y_buf[:, ybase:ybase + 1], y_buf[:, ybase + TC:ybase + TC + 1]))(cb, ybase),
                      reads=(f"y{cb}",), writes=(f"y{cb}c",))
            init = 0.0 if g == 0 else None
            p.add("V", (lambda cb, ybase, init: lambda e: e.tensor_tensor_scan(
                out=y_buf[:, ybase + 1:ybase + 1 + TC],
                data0=s_dec8[:, cb:cb + 1].broadcast_to((P, TC)),
                data1=kv_buf[:, cb * TC:(cb + 1) * TC],
                initial=(y_buf[:, ybase:ybase + 1] if init is None else 0.0),
                op0=ALU.mult, op1=ALU.add))(cb, ybase, init),
                  reads=("sb:s_dec8", f"kv{cb}", f"y{cb}c", "y_init"),
                  writes=(f"y{cb}",))
            if with_wkv:
                sl = cb % 2
                p.add("V", (lambda cb, sl: lambda e: e.tensor_scalar(
                    out=wkvf[sl][:], in0=kv_buf[:, cb * TC:(cb + 1) * TC],
                    scalar1=s_eu8[:, cb:cb + 1], scalar2=None, op0=ALU.mult))(cb, sl),
                      reads=(f"kv{cb}", "sb:s_eu8"), writes=(f"wkvf{sl}",))
                p.add("V", (lambda cb, ybase, sl: lambda e: e.tensor_tensor(
                    out=wkvf[sl][:], in0=wkvf[sl][:],
                    in1=y_buf[:, ybase:ybase + TC], op=ALU.add))(cb, ybase, sl),
                      reads=(f"wkvf{sl}", f"y{cb}", f"y{cb}c", "y_init"),
                      writes=(f"wkvf{sl}",))
                p.add("V", (lambda cb, sl: lambda e: e.tensor_tensor(
                    out=outr[:, cb * TC:(cb + 1) * TC], in0=wkvf[sl][:],
                    in1=r_sb[:, cb * TC:(cb + 1) * TC], op=ALU.mult))(cb, sl),
                      reads=(f"wkvf{sl}", "rsball"), writes=("outr",))

    def r_mms(g):
        par = g % 2
        for eb in range(CB):
            rbank = 0 + (eb % 2)
            for db in range(DBLK):
                p.add("T", (lambda par, eb, db, rbank: lambda e: e.matmul(
                    pf[rbank][:, 0:TC],
                    w_bf["wr"][:, db * 1024 + eb * P: db * 1024 + (eb + 1) * P],
                    xnt_buf[par][:, db * TC:(db + 1) * TC],
                    start=(db == 0), stop=(db == DBLK - 1)))(par, eb, db, rbank),
                      reads=(f"wbf:wr:{db}", f"xnt{par}d{db}"),
                      writes=(f"pf{rbank}",))
            p.add("S", (lambda eb, rbank: lambda e: e.activation(
                r_sb[:, eb * TC:(eb + 1) * TC], pf[rbank][:, 0:TC],
                AF.Sigmoid))(eb, rbank),
                  reads=(f"pf{rbank}",), writes=("rsball",))

    def quadrants_ffn(g):
        par = g % 2
        for q, (tb, oc) in enumerate([(0, 0), (0, 1), (1, 0), (1, 1)]):
            fbank = 4 + oc
            for db in range(DBLK):
                p.add("T", (lambda par, tb, oc, db, fbank: lambda e: e.matmul(
                    pf[fbank][:],
                    xnt_buf[par][:, db * TC + tb * P: db * TC + (tb + 1) * P],
                    w_bf["wm"][:, db * 1024 + oc * 512: db * 1024 + (oc + 1) * 512],
                    start=(db == 0), stop=False))(par, tb, oc, db, fbank),
                      reads=(f"xnt{par}d{db}", f"wbf:wm:{db}"),
                      writes=(f"pf{fbank}",))
            p.add("T", (lambda tb, oc, fbank: lambda e: e.matmul(
                pf[fbank][:], s_ones1[:],
                s_biasrow[0:1, oc * 512:(oc + 1) * 512],
                start=False, stop=True))(tb, oc, fbank),
                  reads=("sb:s_ones1", "sb:s_biasrow"),
                  writes=(f"pf{fbank}",))
            sl = oc
            p.add("S", (lambda fbank, sl: lambda e: e.activation(
                sg_sb[sl][:], pf[fbank][:], AF.Sigmoid))(fbank, sl),
                  reads=(f"pf{fbank}",), writes=(f"sg{sl}",))
            p.add("V", (lambda fbank, sl, q: lambda e: e.tensor_tensor(
                out=gt_sb[q][:], in0=pf[fbank][:], in1=sg_sb[sl][:],
                op=ALU.mult))(fbank, sl, q),
                  reads=(f"pf{fbank}", f"sg{sl}"), writes=(f"gt{q}",))
    def quadrants_wo(g):
        par = g % 2
        xpar = g % 3
        c = g - NCH
        for q, (tb, oc) in enumerate([(0, 0), (0, 1), (1, 0), (1, 1)]):
            bank = 2 + oc
            for eb in range(CB):
                p.add("T", (lambda par, tb, oc, eb, bank: lambda e: e.matmul(
                    pf[bank][:],
                    outr[:, eb * TC + tb * P: eb * TC + (tb + 1) * P],
                    w_bf["wo"][:, eb * 1024 + oc * 512: eb * 1024 + (oc + 1) * 512],
                    start=(eb == 0), stop=(eb == CB - 1)))(par, tb, oc, eb, bank),
                      reads=("outr", f"wbf:wo:{eb}"),
                      writes=(f"pf{bank}",))
            p.add("V", (lambda bank, q: lambda e: e.tensor_tensor(
                out=gt_sb[q][:], in0=pf[bank][:], in1=gt_sb[q][:],
                op=ALU.mult))(bank, q),
                  reads=(f"pf{bank}", f"gt{q}"), writes=(f"gt{q}",))
            p.add("V", (lambda xpar, tb, oc, q: lambda e: e.tensor_tensor(
                out=x_buf[xpar][:, tb * 1024 + oc * 512: tb * 1024 + (oc + 1) * 512],
                in0=gt_sb[q][:],
                in1=x_buf[xpar][:, tb * 1024 + oc * 512: tb * 1024 + (oc + 1) * 512],
                op=ALU.add))(xpar, tb, oc, q),
                  reads=(f"gt{q}", f"x{xpar}"), writes=(f"x{xpar}",))
        p.add("Q", (lambda c, par, xpar: lambda e: e.dma_start(
            out=d_y[c * TC:(c + 1) * TC, :].rearrange("(tb p) d -> p tb d", p=P),
            in_=x_buf[xpar][:].rearrange("p (tb d) -> p tb d", tb=2)))(c, par, xpar),
              reads=(f"x{xpar}",), writes=(f"out{c}",), stream=f"qo{par}")

    def back_warm(g):
        transposes(g)
        kv_mms(g, True)
        stream_weight_dblk("wr", d_wr, g)
        stream_weight_dblk("wo", d_wo, g)
        emit_mix_jobs(8)
        scans(g, with_wkv=False)

    def head_main(g):
        transposes(g)
        kv_mms(g, False)
        r_mms(g)

    load_x(0)
    load_x(1)
    stream_weight("wk", d_wk)
    stream_weight("wv", d_wv)
    stats(0)
    stats(1)
    rstd_fin_pair()
    norm1(0)
    norm1(1)
    for g in range(NCH):
        stats(g + 2)
        if (g + 2) % 2 == 1:
            rstd_fin_pair()
        norm1(g + 1) if g + 1 >= 2 else None
        back_warm(g)
    head_main(NCH)
    for g in range(NCH, NG):
        stats(g + 2)
        if g + 2 < NG and (g + 2) % 2 == 1:
            rstd_fin_pair()
        quadrants_ffn(g)
        norm1(g + 1)
        scans(g, with_wkv=True)
        if g + 1 < NG:
            head_main(g + 1)
        quadrants_wo(g)

    p.add("G", lambda e: e.engine_nop(),
          reads=tuple(f"out{c}" for c in range(NCH)), writes=())

    p.finalize()

    with nc.Block() as block:
        @block.tensor
        def _(eng):
            p.emit_engine("T", eng, sems)

        @block.vector
        def _(eng):
            p.emit_engine("V", eng, sems)

        @block.scalar
        def _(eng):
            p.emit_engine("S", eng, sems)

        @block.sync
        def _(eng):
            p.emit_engine("Q", eng, sems)

        @block.gpsimd
        def _(eng):
            p.emit_engine("G", eng, sems)

    es.close()
    return nc


_NC_CACHE = None


def kernel(L0_attn_x, L0_attn_Wk, L0_attn_Wv, L0_attn_Wr, L0_attn_Wo,
           L0_attn_time_decay, L0_attn_time_first,
           L0_ffn_W, L0_ffn_b, L0_ffn_ids, L0_ffn_weights):
    global _NC_CACHE
    x = np.asarray(L0_attn_x, np.float32)
    wkT = np.ascontiguousarray(np.asarray(L0_attn_Wk, np.float32).T)
    wvT = np.ascontiguousarray(np.asarray(L0_attn_Wv, np.float32).T)
    wrT = np.ascontiguousarray(np.asarray(L0_attn_Wr, np.float32).T)
    woT = np.ascontiguousarray(np.asarray(L0_attn_Wo, np.float32).T)
    ids = np.asarray(L0_ffn_ids).astype(np.int64)
    wselT = np.ascontiguousarray(
        np.asarray(L0_ffn_W, np.float32)[ids].transpose(0, 2, 1))
    bsel = np.ascontiguousarray(np.asarray(L0_ffn_b, np.float32)[ids])
    selw = np.ascontiguousarray(
        np.broadcast_to(np.asarray(L0_ffn_weights, np.float32), (P, 8)))
    selwk = np.ascontiguousarray(
        np.asarray(L0_ffn_weights, np.float32).reshape(8, 1))
    tdec = np.ascontiguousarray(
        np.asarray(L0_attn_time_decay, np.float32).reshape(8, P).T)
    tfir = np.ascontiguousarray(
        np.asarray(L0_attn_time_first, np.float32).reshape(8, P).T)
    identb = np.eye(P, dtype=np.float32).astype(ml_dtypes.bfloat16)
    ones1 = np.ones((1, P), dtype=ml_dtypes.bfloat16)

    if _NC_CACHE is None:
        _NC_CACHE = build_program()
    nc = _NC_CACHE

    in_maps = []
    for c in range(NCORE):
        b, h = c // 2, c % 2
        in_maps.append({
            "x": np.ascontiguousarray(x[b, h * TOK:(h + 1) * TOK, :]),
            "xw": np.ascontiguousarray(x[b, 0:TOK, :]),
            "wk": wkT, "wv": wvT, "wr": wrT, "wo": woT,
            "wsel": wselT, "bsel": bsel, "selw": selw, "selwk": selwk,
            "tdec": tdec, "tfir": tfir,
            "identb": identb, "ones1": ones1,
            "hmask": np.full((P, 1), float(h), np.float32),
        })

    res = run_bass_kernel_spmd(nc, in_maps, core_ids=list(range(NCORE)))
    out = np.empty((4, 4096, D), np.float32)
    for c in range(NCORE):
        b, h = c // 2, c % 2
        out[b, h * TOK:(h + 1) * TOK, :] = np.asarray(res.results[c]["y"])
    return out
